# version 19
# baseline (speedup 1.0000x reference)
import contextlib
import numpy as np
import concourse.bass as bass
import concourse.mybir as mybir
from concourse.bass_utils import run_bass_kernel_spmd

F32 = mybir.dt.float32
BF16 = mybir.dt.bfloat16
I8 = mybir.dt.int8
AF = mybir.ActivationFunctionType
ALU = mybir.AluOpType
AX = mybir.AxisListType

D = 2048
DI = 8192
NCORES = 8
LN_EPS = 1e-5
ATT_EPS = 1e-9
HD_SCALE = float(128.0 ** -0.5)


class Ctr:
    def __init__(self, sem):
        self.sem = sem
        self.n = 0

    def bump(self, k):
        self.n += k
        return self.n


class Rec:
    def __init__(self, name):
        self.name = name
        self.items = []

    def wait(self, ctr, val):
        assert 0 <= val <= ctr.n, \
            f"{self.name} waits {val} but ctr max-so-far {ctr.n}"
        if val == 0:
            return
        sem = ctr.sem
        self.items.append(lambda e: e.wait_ge(sem, val))

    def op(self, meth, *args, inc=None, **kw):
        after = None
        sem = val = None
        if inc is not None:
            ctr, amt = inc
            after = ctr.bump(amt)
            sem, val = ctr.sem, amt

        def emit(e, meth=meth, args=args, kw=kw, sem=sem, val=val):
            ins = getattr(e, meth)(*args, **kw)
            if sem is not None:
                ins.then_inc(sem, val)
        self.items.append(emit)
        return after

    def replay(self, e):
        for f in self.items:
            f(e)


def build_nc(T):
    NTT = T // 128
    NCH = T // 512
    assert T % 512 == 0

    nc = bass.Bass("TRN2", target_bir_lowering=False, debug=False)

    x_in = nc.dram_tensor("x_sh", [T, D], F32, kind="ExternalInput").ap()
    wqkvT = nc.dram_tensor("wqkvT", [D, 3 * D], F32, kind="ExternalInput").ap()
    woutT = nc.dram_tensor("woutT", [D, D], F32, kind="ExternalInput").ap()
    wg_in = nc.dram_tensor("wg", [D, DI], I8, kind="ExternalInput").ap()
    wu_in = nc.dram_tensor("wu", [D, DI], I8, kind="ExternalInput").ap()
    wd_in = nc.dram_tensor("wd", [DI, D], I8, kind="ExternalInput").ap()
    sg_in = nc.dram_tensor("sg_c", [128, DI // 128], F32, kind="ExternalInput").ap()
    su_in = nc.dram_tensor("su_c", [128, DI // 128], F32, kind="ExternalInput").ap()
    sd_in = nc.dram_tensor("sd_row", [1, D], F32, kind="ExternalInput").ap()
    u_in = nc.dram_tensor("u_tri", [128, 128], BF16, kind="ExternalInput").ap()
    ones1_in = nc.dram_tensor("ones1f", [1, 128], F32, kind="ExternalInput").ap()
    SG = NCORES * NTT
    x_full = nc.dram_tensor("x_full", [(NCORES - 1) * T, D], F32,
                            kind="ExternalInput").ap()
    ps_in = nc.dram_tensor("ps_sel", [SG, NTT], F32, kind="ExternalInput").ap()

    y_out = nc.dram_tensor("y_out", [T, D], F32, kind="ExternalOutput").ap()

    xn_st = nc.dram_tensor("xn_st", [T, D], BF16).ap()
    xn_st2 = nc.dram_tensor("xn_st2", [T, D], BF16).ap()
    q_st = nc.dram_tensor("q_st", [T, D], BF16).ap()
    k_st = nc.dram_tensor("k_st", [T, D], BF16).ap()
    kv_st = nc.dram_tensor("kv_st", [T, D], BF16).ap()
    ts_st = nc.dram_tensor("ts_st", [2, NCORES * NTT, D], F32).ap()
    car_st = nc.dram_tensor("car_st", [2, NTT, D], F32).ap()
    car_hi = nc.dram_tensor("car_hi", [2, NTT, D], BF16).ap()
    car_lo = nc.dram_tensor("car_lo", [2, NTT, D], BF16).ap()
    out_st = nc.dram_tensor("out_st", [T, D], BF16).ap()
    y_st = nc.dram_tensor("y_st", [T, D], F32).ap()
    yn_st = nc.dram_tensor("yn_st", [T, D], BF16).ap()

    recs = {k: Rec(k) for k in ("sync", "tensor", "scalar", "vector", "gpsimd")}
    SY, TE, SC, VE, GP = (recs[k] for k in
                          ("sync", "tensor", "scalar", "vector", "gpsimd"))

    es = contextlib.ExitStack()
    with es:
        sem_names = ["dx0", "dx1", "dxn0", "dxn1", "dxnT", "dw0", "dw1",
                     "dstK", "dstKV", "dstQ", "dts0", "dts1", "dcar",
                     "dsr0", "dsr1", "dcr", "dxr", "dyr",
                     "dout0", "dout1", "doutT", "dy", "dyn", "dynT",
                     "dwg0", "dwg1", "dwd0", "dwd1", "dfin",
                     "mqkv", "mts", "mev", "mtse", "v15",
                     "mcs", "fv", "fs", "mat", "sat", "vat", "mg", "sfa", "vfa",
                     "mzb", "vz", "dmisc", "dwo", "fg", "scv", "dsg0", "dsg1", "dsr2", "dsg2", "dsd"]
        C = {}
        for s in sem_names:
            C[s] = Ctr(es.enter_context(nc.semaphore(s)))

        def phase_barrier(ctrs):
            for R in (SY, TE, SC, VE, GP):
                for cname in ctrs:
                    R.wait(C[cname], C[cname].n)

        u_sb = es.enter_context(nc.sbuf_tensor("u_sb", [128, 128], BF16))
        ps_sb = es.enter_context(nc.sbuf_tensor("ps_sb", [SG, NTT], F32))
        ones1_sb = es.enter_context(nc.sbuf_tensor("ones1_sb", [1, 128], F32))
        sg_sb = es.enter_context(nc.sbuf_tensor("sg_sb", [128, DI // 128], F32))
        su_sb = es.enter_context(nc.sbuf_tensor("su_sb", [128, DI // 128], F32))
        for dst, src in [(u_sb, u_in), (ps_sb, ps_in),
                         (ones1_sb, ones1_in), (sg_sb, sg_in), (su_sb, su_in)]:
            SY.op("dma_start", out=dst[:, :], in_=src, inc=(C["dmisc"], 16))
        n_const = C["dmisc"].n
        for R in (TE, SC, VE, GP):
            R.wait(C["dmisc"], n_const)

        lnt = {}
        for nmx in ("s1", "s2", "mean", "v2", "m2", "var", "sd", "rstd", "nmr"):
            lnt[nmx] = es.enter_context(nc.sbuf_tensor("ln_" + nmx, [128, 1], F32))
        sq_scr = es.enter_context(nc.sbuf_tensor("sq_scr", [128, D], BF16))

        def ln_stats_apply(x_ap, out_ap):
            a = VE.op("reduce_sum", lnt["s1"][:, 0:1], x_ap, axis=AX.X,
                      inc=(C["fv"], 1))
            b = VE.op("scalar_tensor_tensor", sq_scr[:, :], x_ap, 1.0, x_ap,
                      ALU.mult, ALU.mult, accum_out=lnt["s2"][:, 0:1],
                      inc=(C["fv"], 1))
            VE.wait(C["fv"], a)
            c = VE.op("tensor_scalar_mul", lnt["mean"][:, 0:1], lnt["s1"][:, 0:1],
                      1.0 / D, inc=(C["fv"], 1))
            VE.wait(C["fv"], b)
            d = VE.op("tensor_scalar_mul", lnt["v2"][:, 0:1], lnt["s2"][:, 0:1],
                      1.0 / D, inc=(C["fv"], 1))
            VE.wait(C["fv"], c)
            e = VE.op("tensor_mul", lnt["m2"][:, 0:1], lnt["mean"][:, 0:1],
                      lnt["mean"][:, 0:1], inc=(C["fv"], 1))
            VE.wait(C["fv"], e)
            f = VE.op("tensor_sub", lnt["var"][:, 0:1], lnt["v2"][:, 0:1],
                      lnt["m2"][:, 0:1], inc=(C["fv"], 1))
            SC.wait(C["fv"], f)
            g = SC.op("activation", lnt["sd"][:, 0:1], lnt["var"][:, 0:1], AF.Ln,
                      bias=np.float32(LN_EPS), inc=(C["fs"], 1))
            SC.wait(C["fs"], g)
            h = SC.op("activation", lnt["rstd"][:, 0:1], lnt["sd"][:, 0:1],
                      AF.Exp, scale=np.float32(-0.5), inc=(C["fs"], 1))
            VE.wait(C["fs"], h)
            i = VE.op("scalar_tensor_tensor", lnt["nmr"][:, 0:1],
                      lnt["mean"][:, 0:1], -1.0, lnt["rstd"][:, 0:1],
                      ALU.mult, ALU.mult, inc=(C["fv"], 1))
            SC.wait(C["fv"], i)
            j = SC.op("activation", out_ap, x_ap, AF.Identity,
                      scale=lnt["rstd"][:, 0:1], bias=lnt["nmr"][:, 0:1],
                      inc=(C["fs"], 1))
            return j


        def p0_load(x_src, row0, tt, xbuf, ln_done, dxp2, dxnp2):
            par = tt % 2
            dxp = dxp2[par]
            if ln_done[par] > 0:
                SY.wait(C["fs"], ln_done[par])
                SY.wait(dxnp2[par], dxnp2[par].n)
            SY.op("dma_start", out=xbuf[par][:, :],
                  in_=x_src[row0 + tt * 128:row0 + (tt + 1) * 128, :],
                  inc=(dxp, 16))
            return dxp.n

        def p0_stats(tt, xbuf, dxp2, nload):
            par = tt % 2
            VE.wait(dxp2[par], nload)
            a = VE.op("reduce_sum", lnt["s1"][:, 0:1], xbuf[par][:, :],
                      axis=AX.X, inc=(C["fv"], 1))
            b = VE.op("scalar_tensor_tensor", sq_scr[:, :], xbuf[par][:, :],
                      1.0, xbuf[par][:, :], ALU.mult, ALU.mult,
                      accum_out=lnt["s2"][:, 0:1], inc=(C["fv"], 1))
            return a, b

        def p0_chain(ab):
            a, b = ab
            VE.wait(C["fv"], a)
            c = VE.op("tensor_scalar_mul", lnt["mean"][:, 0:1], lnt["s1"][:, 0:1],
                      1.0 / D, inc=(C["fv"], 1))
            VE.wait(C["fv"], b)
            d = VE.op("tensor_scalar_mul", lnt["v2"][:, 0:1], lnt["s2"][:, 0:1],
                      1.0 / D, inc=(C["fv"], 1))
            VE.wait(C["fv"], c)
            e = VE.op("tensor_mul", lnt["m2"][:, 0:1], lnt["mean"][:, 0:1],
                      lnt["mean"][:, 0:1], inc=(C["fv"], 1))
            VE.wait(C["fv"], e)
            f = VE.op("tensor_sub", lnt["var"][:, 0:1], lnt["v2"][:, 0:1],
                      lnt["m2"][:, 0:1], inc=(C["fv"], 1))
            SC.wait(C["fv"], f)
            g = SC.op("activation", lnt["sd"][:, 0:1], lnt["var"][:, 0:1], AF.Ln,
                      bias=np.float32(LN_EPS), inc=(C["fs"], 1))
            SC.wait(C["fs"], g)
            h = SC.op("activation", lnt["rstd"][:, 0:1], lnt["sd"][:, 0:1],
                      AF.Exp, scale=np.float32(-0.5), inc=(C["fs"], 1))
            return h

        def p0_apply(tt, dst_st, xbuf, xnb, ln_done, dxp2, dxnp2, h):
            par = tt % 2
            VE.wait(C["fs"], h)
            i = VE.op("scalar_tensor_tensor", lnt["nmr"][:, 0:1],
                      lnt["mean"][:, 0:1], -1.0, lnt["rstd"][:, 0:1],
                      ALU.mult, ALU.mult, inc=(C["fv"], 1))
            SC.wait(C["fv"], i)
            j = SC.op("activation", xnb[par][:, :], xbuf[par][:, :], AF.Identity,
                      scale=lnt["rstd"][:, 0:1], bias=lnt["nmr"][:, 0:1],
                      inc=(C["fs"], 1))
            ln_done[par] = j
            SY.wait(C["fs"], j)
            SY.op("dma_start", out=dst_st[tt * 128:(tt + 1) * 128, :],
                  in_=xnb[par][:, :], inc=(dxnp2[par], 16))

        def p0_compute(tt, dst_st, xbuf, xnb, ln_done, dxp2, dxnp2, nload):
            ab = p0_stats(tt, xbuf, dxp2, nload)
            h = p0_chain(ab)
            p0_apply(tt, dst_st, xbuf, xnb, ln_done, dxp2, dxnp2, h)

        def p0_tile(x_src, row0, tt, dst_st, xbuf, xnb, ln_done, dxp2, dxnp2):
            n = p0_load(x_src, row0, tt, xbuf, ln_done, dxp2, dxnp2)
            p0_compute(tt, dst_st, xbuf, xnb, ln_done, dxp2, dxnp2, n)

        def emit_p0_block(x_src, row0, dst_st, xbuf, xnb, ln_done, dxp2, dxnp2):
            for tt in range(NTT):
                p0_tile(x_src, row0, tt, dst_st, xbuf, xnb, ln_done, dxp2, dxnp2)

        class P1State:
            def __init__(self):
                self.gidx = 0
                self.ts_group = 0
                self.pending_ts = []

        mev_hist = {}

        def emit_qkv_pass(st, jt_list, xnT, wqb, kbuf, kvbuf, qbuf, r2b,
                          tsrow, ps_qkv, ps_ts, do_stash, ts_base, bg=None,
                          tail_hook=None):
            for ji0, (kind, jt) in enumerate(jt_list):
                js = jt * 512
                col = js - {"k": 2048, "v": 4096, "q": 0}[kind]
                dwp = C["dw0"] if st.gidx // NTT % 2 == 0 else C["dw1"]
                par_w = (st.gidx // NTT) % 2
                if st.gidx // NTT >= 2:
                    GP.wait(C["mqkv"], st.gidx - NTT)
                srcw = wqkvT.rearrange("(dc p) j -> p dc j", p=128)[
                    :, :, js:js + 512]
                GP.op("dma_start",
                      out=wqb[par_w][:, :].rearrange("p (dc j) -> p dc j", j=512),
                      in_=srcw, inc=(dwp, 16))

                if kind == "k":
                    VE.wait(C["dstK"], C["dstK"].n)
                    VE.wait(C["mts"], C["mts"].n)
                if kind == "v":
                    VE.wait(C["dstKV"], C["dstKV"].n)
                    VE.wait(C["mts"], C["mts"].n)
                if kind == "q":
                    VE.wait(C["dstQ"], C["dstQ"].n)

                for tt in range(NTT):
                    par = st.gidx % 4
                    TE.wait(dwp, dwp.n)
                    TE.wait(C["dxnT"], C["dxnT"].n)
                    if st.gidx >= 4:
                        TE.wait(C["mev"], st.gidx - 3)
                    for dc in range(16):
                        TE.op("matmul", ps_qkv[par][:, :],
                              xnT[:, dc * T + tt * 128: dc * T + (tt + 1) * 128],
                              wqb[par_w][:, dc * 512:(dc + 1) * 512],
                              start=(dc == 0), stop=(dc == 15),
                              inc=(C["mqkv"], 1) if dc == 15 else None)
                    mq_now = C["mqkv"].n

                    if kind in ("k", "q"):
                        SC.wait(C["mqkv"], mq_now)
                        if st.gidx >= 4:
                            SC.wait(C["mev"], st.gidx - 3)
                        a = SC.op("activation", r2b[par][:, :], ps_qkv[par][:, :],
                                  AF.Relu, scale=-1.0, inc=(C["fs"], 1))
                        SC.wait(C["fs"], a)
                        b = SC.op("activation", r2b[par][:, :], r2b[par][:, :],
                                  AF.Exp, scale=-1.0, inc=(C["fs"], 1))
                        VE.wait(C["fs"], b)
                        buf = kbuf if kind == "k" else qbuf
                        VE.op("scalar_tensor_tensor",
                              buf[:, tt * 512:(tt + 1) * 512],
                              ps_qkv[par][:, :], 0.0, r2b[par][:, :],
                              ALU.max, ALU.add, inc=(C["mev"], 1))
                    else:
                        VE.wait(C["mqkv"], mq_now)
                        VE.wait(C["mev"], st.gidx - NTT + 1)
                        VE.op("scalar_tensor_tensor",
                              kvbuf[:, tt * 512:(tt + 1) * 512],
                              ps_qkv[par][:, :], 1.0,
                              kbuf[:, tt * 512:(tt + 1) * 512],
                              ALU.mult, ALU.mult, inc=(C["mev"], 1))
                    mev_now = C["mev"].n

                    mev_hist[st.gidx] = mev_now
                    st.gidx += 1
                    if st.pending_ts and st.gidx % 2 == 1:
                        st.pending_ts.pop(0)()
                    if bg and st.gidx % 2 == 0:
                        bg.pop(0)()

                if tail_hook is not None and ji0 == len(jt_list) - 1:
                    tail_hook()
                    tail_hook = None
                if ts_base is not None and kind in ("k", "v"):
                    tbuf = kbuf if kind == "k" else kvbuf
                    for t0 in range(0, NTT, 2):
                        def ts_group_emit(t0=t0, tbuf=tbuf, kind=kind, col=col,
                                          base=ts_base,
                                          mevs=(mev_hist[st.gidx - NTT + t0],
                                                mev_hist[st.gidx - NTT + t0 + 1])):
                            gpar = st.ts_group % 2
                            for j, tt in enumerate((t0, t0 + 1)):
                                TE.wait(C["mev"], mevs[j])
                                if j == 0 and st.ts_group >= 2:
                                    TE.wait(C["mtse"], st.ts_group - 1)
                                TE.op("matmul",
                                      ps_ts[gpar][0:1, j * 512:j * 512 + 512],
                                      u_sb[:, 127:128],
                                      tbuf[:, tt * 512:(tt + 1) * 512],
                                      start=True, stop=True, inc=(C["mts"], 1))
                            dtsp = C["dts0"] if gpar == 0 else C["dts1"]
                            SC.wait(C["mts"], C["mts"].n)
                            SC.wait(dtsp, dtsp.n)
                            ev = SC.op("copy", tsrow[gpar][0:1, :],
                                       ps_ts[gpar][0:1, :], inc=(C["mtse"], 1))
                            st.ts_group += 1
                            SY.wait(C["mtse"], ev)
                            idx = 0 if kind == "k" else 1
                            SY.op("dma_start",
                                  out=ts_st[idx:idx + 1, base + t0:base + t0 + 2,
                                            col:col + 512],
                                  in_=tsrow[gpar][0:1, :].rearrange(
                                      "p (g j) -> p g j", j=512),
                                  inc=(dtsp, 16))
                        st.pending_ts.append(ts_group_emit)

                if ji0 == len(jt_list) - 1:
                    while st.pending_ts:
                        st.pending_ts.pop(0)()
                if do_stash:
                    buf = {"k": kbuf, "v": kvbuf, "q": qbuf}[kind]
                    stg = {"k": k_st, "v": kv_st, "q": q_st}[kind]
                    SY.wait(C["mev"], C["mev"].n)
                    stsem = {"k": C["dstK"], "v": C["dstKV"],
                             "q": C["dstQ"]}[kind]
                    SY.op("dma_start",
                          out=stg.rearrange("(tt p) c -> p tt c", p=128)[
                              :, :, col:col + 512],
                          in_=buf[:, :].rearrange("p (tt j) -> p tt j", j=512),
                          inc=(stsem, 16))

        with contextlib.ExitStack() as p1:
            xbuf = [p1.enter_context(nc.sbuf_tensor(f"xb{i}", [128, D], F32))
                    for i in range(2)]
            xnb = [p1.enter_context(nc.sbuf_tensor(f"xnb{i}", [128, D], BF16))
                   for i in range(2)]
            xnT = p1.enter_context(nc.sbuf_tensor("xnT", [128, 16 * T], BF16))
            wqb = [p1.enter_context(
                nc.sbuf_tensor(f"wqb{i}", [128, 16 * 512], BF16))
                for i in range(2)]
            kbuf = p1.enter_context(nc.sbuf_tensor("kbuf", [128, NTT * 512], BF16))
            kvbuf = p1.enter_context(
                nc.sbuf_tensor("kvbuf", [128, NTT * 512], BF16))
            qbuf = p1.enter_context(nc.sbuf_tensor("qbuf", [128, NTT * 512], BF16))
            r2b = [p1.enter_context(nc.sbuf_tensor(f"r2b{i}", [128, 512], F32))
                   for i in range(4)]
            tsrow = [p1.enter_context(nc.sbuf_tensor(f"tsrow{i}", [1, 1024], F32))
                     for i in range(2)]
            ps_qkv = [p1.enter_context(nc.psum_tensor(f"psq{i}", [128, 512], F32))
                      for i in range(4)]
            ps_ts = [p1.enter_context(nc.psum_tensor(f"psts{i}", [1, 1024], F32))
                     for i in range(2)]

            st = P1State()
            ln_done = [0, 0]
            dxp2 = [C["dx0"], C["dx1"]]
            dxnp2 = [C["dxn0"], C["dxn1"]]

            own_jts = []
            for i in range(4):
                own_jts.append(("k", 4 + i))
                own_jts.append(("v", 8 + i))
            for i in range(4):
                own_jts.append(("q", i))
            full_jts = []
            for i in range(4):
                full_jts.append(("k", 4 + i))
                full_jts.append(("v", 8 + i))

            xst = [xn_st, xn_st2]

            def mk_bg(blk):
                items = []
                ndxnT = C["dxnT"].n
                state = {}

                def f_stats(k):
                    if k == 0:
                        SY.wait(C["dxnT"], ndxnT)
                        state[("ld", 0)] = p0_load(x_full, blk * T, 0, xbuf,
                                                   ln_done, dxp2, dxnp2)
                    if k + 1 < NTT:
                        state[("ld", k + 1)] = p0_load(x_full, blk * T, k + 1,
                                                       xbuf, ln_done, dxp2,
                                                       dxnp2)
                    state[("ab", k)] = p0_stats(k, xbuf, dxp2,
                                                state[("ld", k)])

                def f_chain(k):
                    state[("h", k)] = p0_chain(state[("ab", k)])

                def f_apply(k):
                    p0_apply(k, xst[blk % 2], xbuf, xnb, ln_done, dxp2, dxnp2,
                             state[("h", k)])

                for tt in range(NTT):
                    items.append(lambda k=tt: f_stats(k))
                    items.append(lambda k=tt: f_chain(k))
                    items.append(lambda k=tt: f_apply(k))
                return items

            emit_p0_block(x_in, 0, xn_st, xbuf, xnb, ln_done, dxp2, dxnp2)
            SC.wait(C["dxn0"], C["dxn0"].n)
            SC.wait(C["dxn1"], C["dxn1"].n)
            for dc in range(16):
                SC.op("dma_start_transpose", xnT[:, dc * T:(dc + 1) * T],
                      xn_st[:, dc * 128:(dc + 1) * 128], inc=(C["dxnT"], 16))
            bg = mk_bg(0)
            emit_qkv_pass(st, own_jts, xnT, wqb, kbuf, kvbuf, qbuf, r2b,
                          tsrow, ps_qkv, ps_ts, do_stash=True, ts_base=0,
                          bg=bg)
            assert not bg

            def xnT_transposes(blk):
                SC.wait(C["dxn0"], C["dxn0"].n)
                SC.wait(C["dxn1"], C["dxn1"].n)
                SC.wait(C["mqkv"], C["mqkv"].n)
                for dc in range(16):
                    SC.op("dma_start_transpose", xnT[:, dc * T:(dc + 1) * T],
                          xst[blk % 2][:, dc * 128:(dc + 1) * 128],
                          inc=(C["dxnT"], 16))

            NBLK = NCORES - 1
            xnT_transposes(0)
            for blk in range(NBLK):
                bg = mk_bg(blk + 1) if blk + 1 < NBLK else []
                hook = (lambda b=blk: xnT_transposes(b + 1)) \
                    if blk + 1 < NBLK else None
                emit_qkv_pass(st, full_jts, xnT, wqb, kbuf, kvbuf, qbuf,
                              r2b, tsrow, ps_qkv, ps_ts, do_stash=False,
                              ts_base=(blk + 1) * NTT, bg=bg, tail_hook=hook)
                for f in bg:
                    f()

            n_stash = C["dstK"].n + C["dstKV"].n + C["dstQ"].n
            n_mqkv = C["mqkv"].n
            n_dts = C["dts0"].n + C["dts1"].n
            n_mev = C["mev"].n
            n_mtse = C["mtse"].n
            assert C["mqkv"].n == (12 + 8 * (NCORES - 1)) * NTT
            assert n_dts == 16 * NCORES * 8 * (NTT // 2)

        phase_barrier(["dw0", "dw1", "dstK", "dstKV", "dstQ", "dts0", "dts1",
                       "mqkv", "mev", "mtse", "fs", "fv", "dxnT", "dx0", "dx1",
                       "dxn0", "dxn1"])

        with contextlib.ExitStack() as p15:
            ts_sb = [p15.enter_context(nc.sbuf_tensor(f"tssb{i}", [SG, D], F32))
                     for i in range(2)]
            carf = [p15.enter_context(nc.sbuf_tensor(f"carf{i}", [NTT, D], F32))
                    for i in range(2)]
            carh = [p15.enter_context(nc.sbuf_tensor(f"carh{i}", [NTT, D], BF16))
                    for i in range(2)]
            carl = [p15.enter_context(nc.sbuf_tensor(f"carl{i}", [NTT, D], BF16))
                    for i in range(2)]
            SY.wait(C["dts0"], C["dts0"].n)
            SY.wait(C["dts1"], C["dts1"].n)
            for i in range(2):
                SY.op("dma_start", out=ts_sb[i][:, :], in_=ts_st[i, :, :],
                      inc=(C["dcar"], 16))
            TE.wait(C["mev"], n_mev)
            TE.wait(C["mtse"], n_mtse)
            with nc.psum_tensor("ps_pre0", [NTT, D], F32) as pp0, \
                    nc.psum_tensor("ps_pre1", [NTT, D], F32) as pp1:
                ps_pre = [pp0, pp1]
                for i in range(2):
                    TE.wait(C["dcar"], 32)
                    for sl in range(4):
                        TE.op("matmul", ps_pre[i][:, sl * 512:(sl + 1) * 512],
                              ps_sb[:, :], ts_sb[i][:, sl * 512:(sl + 1) * 512],
                              start=True, stop=True,
                              inc=(C["mts"], 1) if sl == 3 else None)
                    VE.wait(C["mts"], C["mts"].n)
                    v2 = VE.op("tensor_copy", carf[i][:, :], ps_pre[i][:, :],
                               inc=(C["v15"], 1))
                    VE.wait(C["v15"], v2)
                    v2h = VE.op("tensor_copy", carh[i][:, :], ps_pre[i][:, :],
                                inc=(C["v15"], 1))
                    VE.wait(C["v15"], v2h)
                    v2l = VE.op("tensor_sub", carl[i][:, :], ps_pre[i][:, :],
                                carh[i][:, :], inc=(C["v15"], 1))
                    SY.wait(C["v15"], v2l)
                    SY.op("dma_start", out=car_hi[i, :, :], in_=carh[i][:, :],
                          inc=(C["dcar"], 16))
                    SY.op("dma_start", out=car_lo[i, :, :], in_=carl[i][:, :],
                          inc=(C["dcar"], 16))
            n_dcar = C["dcar"].n
            n_v15 = C["v15"].n
            assert n_v15 == 6

        phase_barrier(["dcar", "v15", "mts"])

        with contextlib.ExitStack() as p2a:
            wob = p2a.enter_context(nc.sbuf_tensor("wob", [128, 16 * D], BF16))
            qt = [p2a.enter_context(nc.sbuf_tensor(f"qt{i}", [128, D], BF16))
                  for i in range(3)]
            kt = [p2a.enter_context(nc.sbuf_tensor(f"kt{i}", [128, D], BF16))
                  for i in range(3)]
            kvt = [p2a.enter_context(nc.sbuf_tensor(f"kvt{i}", [128, D], BF16))
                   for i in range(3)]
            crk = [p2a.enter_context(nc.sbuf_tensor(f"crk{i}", [1, 2 * D], BF16))
                   for i in range(2)]
            crkv = [p2a.enter_context(nc.sbuf_tensor(f"crkv{i}", [1, 2 * D], BF16))
                    for i in range(2)]
            t1s = [p2a.enter_context(nc.sbuf_tensor(f"t1s{i}", [128, 512], F32))
                   for i in range(2)]
            t2s = [p2a.enter_context(nc.sbuf_tensor(f"t2s{i}", [128, 512], F32))
                   for i in range(2)]
            lns = [p2a.enter_context(nc.sbuf_tensor(f"lns{i}", [128, 512], F32))
                   for i in range(2)]
            outb = [p2a.enter_context(nc.sbuf_tensor(f"outb{i}", [128, D], BF16))
                    for i in range(2)]
            outT = p2a.enter_context(nc.sbuf_tensor("outT", [128, 16 * 512], BF16))
            attnf = [p2a.enter_context(nc.sbuf_tensor(f"attnf{i}", [128, D], F32))
                     for i in range(2)]
            xrb = [p2a.enter_context(nc.sbuf_tensor("xrb0", [128, D], F32))] * 2
            ynb = p2a.enter_context(nc.sbuf_tensor("ynb", [128, D], BF16))
            ps_kc = [p2a.enter_context(nc.psum_tensor(f"ps_kc{i}", [128, 512], F32))
                     for i in range(2)]
            ps_kvc = [p2a.enter_context(nc.psum_tensor(f"ps_kvc{i}", [128, 512], F32))
                      for i in range(2)]
            ps_at = [p2a.enter_context(nc.psum_tensor(f"ps_at{i}", [128, 512], F32))
                     for i in range(2)]

            GP.wait(C["mqkv"], n_mqkv)
            GP.wait(C["dstK"], C["dstK"].n)
            GP.wait(C["dstKV"], C["dstKV"].n)
            GP.wait(C["dstQ"], C["dstQ"].n)
            GP.op("dma_start",
                  out=wob[:, :].rearrange("p (fc e) -> p fc e", e=D),
                  in_=woutT.rearrange("(fc p) e -> p fc e", p=128),
                  inc=(C["dwo"], 16))

            TE.wait(C["v15"], n_v15)

            att_G = 0
            dout_hist = {}


            b_hist = {}
            mcs_hist = {}
            vat_tile = {}
            ln_done_q = {}
            mult_done_q = {}
            pend = [None]
            first_mult_of_tile = {}

            def emit_mult(qg, mpar, mc0, dd, b):
                VE.wait(C["fs"], dd)
                VE.wait(C["fv"], b)
                if qg % 4 == 0:
                    mtile = qg // 4
                    if mtile >= 2:
                        doutp = C["dout0"] if mtile % 2 == 0 else C["dout1"]
                        VE.wait(doutp, doutp.n)
                v = VE.op("tensor_mul", outb[mpar][:, mc0:mc0 + 512],
                          t1s[qg % 2][:, :], lns[qg % 2][:, :],
                          inc=(C["vat"], 1))
                mult_done_q[qg] = v

            def emit_X_tile(ch, tl):
                if True:
                    tt = ch * 4 + tl
                    par = tt % 2
                    p3 = tt % 3
                    dsrp = [C["dsr0"], C["dsr1"], C["dsr2"]][p3]
                    dsgp = [C["dsg0"], C["dsg1"], C["dsg2"]][p3]
                    if tt >= 3:
                        SY.wait(C["vat"], vat_tile[tt - 3])
                        GP.wait(C["vat"], vat_tile[tt - 3])
                    GP.op("dma_start", out=qt[p3][:, :],
                          in_=q_st[tt * 128:(tt + 1) * 128, :], inc=(dsgp, 16))
                    SY.op("dma_start", out=kt[p3][:, :],
                          in_=k_st[tt * 128:(tt + 1) * 128, :], inc=(dsrp, 16))
                    GP.op("dma_start", out=kvt[p3][:, :],
                          in_=kv_st[tt * 128:(tt + 1) * 128, :],
                          inc=(dsgp, 16))
                    if tt >= 2:
                        SY.wait(C["mcs"], mcs_hist[tt - 1])
                    SY.op("dma_start", out=crk[par][0:1, 0:D],
                          in_=car_hi[0, tt:tt + 1, :], inc=(C["dcr"], 16))
                    SY.op("dma_start", out=crk[par][0:1, D:2 * D],
                          in_=car_lo[0, tt:tt + 1, :], inc=(C["dcr"], 16))
                    SY.op("dma_start", out=crkv[par][0:1, 0:D],
                          in_=car_hi[1, tt:tt + 1, :], inc=(C["dcr"], 16))
                    SY.op("dma_start", out=crkv[par][0:1, D:2 * D],
                          in_=car_lo[1, tt:tt + 1, :], inc=(C["dcr"], 16))
                    nds = dsrp.n
                    ndsg = dsgp.n
                    ndcr = C["dcr"].n

                    for q in range(4):
                        qg = emit_X_tile.qg
                        qs = qg % 2
                        c0 = q * 512
                        TE.wait(dsrp, nds)
                        TE.wait(dsgp, ndsg)
                        TE.wait(C["dcr"], ndcr)
                        if qg - 2 in b_hist:
                            TE.wait(C["fv"], b_hist[qg - 2])
                        TE.op("matmul", ps_kc[qs][:, :],
                              u_sb[:, :], kt[p3][:, c0:c0 + 512],
                              start=True, stop=False)
                        TE.op("matmul", ps_kc[qs][:, :],
                              u_sb[0:1, :], crk[par][0:1, c0:c0 + 512],
                              start=False, stop=False)
                        TE.op("matmul", ps_kc[qs][:, :],
                              u_sb[0:1, :], crk[par][0:1, D + c0:D + c0 + 512],
                              start=False, stop=True)
                        TE.op("matmul", ps_kvc[qs][:, :],
                              u_sb[:, :], kvt[p3][:, c0:c0 + 512],
                              start=True, stop=False)
                        TE.op("matmul", ps_kvc[qs][:, :],
                              u_sb[0:1, :], crkv[par][0:1, c0:c0 + 512],
                              start=False, stop=False)
                        TE.op("matmul", ps_kvc[qs][:, :],
                              u_sb[0:1, :], crkv[par][0:1, D + c0:D + c0 + 512],
                              start=False, stop=True, inc=(C["mcs"], 1))
                        mnow = C["mcs"].n

                        if pend[0] is not None:
                            emit_mult(*pend[0])
                        VE.wait(C["mcs"], mnow)
                        VE.wait(dsgp, ndsg)
                        if qg - 2 in ln_done_q:
                            VE.wait(C["fs"], ln_done_q[qg - 2])
                        if qg - 2 in mult_done_q:
                            VE.wait(C["vat"], mult_done_q[qg - 2])
                        a = VE.op("scalar_tensor_tensor", t2s[qs][:, :],
                                  ps_kc[qs][:, :], 1.0,
                                  qt[p3][:, c0:c0 + 512],
                                  ALU.mult, ALU.mult, inc=(C["fv"], 1))
                        b = VE.op("scalar_tensor_tensor", t1s[qs][:, :],
                                  ps_kvc[qs][:, :], 1.0,
                                  qt[p3][:, c0:c0 + 512],
                                  ALU.mult, ALU.mult, inc=(C["fv"], 1))
                        b_hist[qg] = b
                        SC.wait(C["fv"], a)
                        if qg - 2 in mult_done_q:
                            SC.wait(C["vat"], mult_done_q[qg - 2])
                        cc = SC.op("activation", lns[qs][:, :], t2s[qs][:, :],
                                   AF.Ln, bias=np.float32(ATT_EPS),
                                   inc=(C["fs"], 1))
                        SC.wait(C["fs"], cc)
                        dd = SC.op("activation", lns[qs][:, :], lns[qs][:, :],
                                   AF.Exp, scale=-1.0, inc=(C["fs"], 1))
                        ln_done_q[qg] = dd
                        pend[0] = (qg, par, c0, dd, b)
                        emit_X_tile.qg += 1
                    mcs_hist[tt] = C["mcs"].n
                    if pend[0] is not None:
                        emit_mult(*pend[0])
                        pend[0] = None
                    vat_tile[tt] = C["vat"].n
                    SY.wait(C["vat"], C["vat"].n)
                    doutp = C["dout0"] if par == 0 else C["dout1"]
                    SY.op("dma_start", out=out_st[tt * 128:(tt + 1) * 128, :],
                          in_=outb[par][:, :], inc=(doutp, 16))
                if tl == 3:
                    dout_hist[ch] = (C["dout0"].n, C["dout1"].n)

            emit_X_tile.qg = 0
            ln_hist = {}
            vy_hist = {}

            def emit_Y_head(ch):
                SC.wait(C["dout0"], dout_hist[ch][0])
                SC.wait(C["dout1"], dout_hist[ch][1])
                for fc in range(16):
                    SC.op("dma_start_transpose", outT[:, fc * 512:(fc + 1) * 512],
                          out_st[ch * 512:(ch + 1) * 512,
                                 fc * 128:(fc + 1) * 128],
                          inc=(C["doutT"], 16))
                emit_Y_head.n_doutT = C["doutT"].n

            def emit_Y_tile(ch, tl):
                n_doutT = emit_Y_head.n_doutT
                if True:
                    tt = ch * 4 + tl
                    VE_done_for_xrb = C["fv"].n
                    SY.wait(C["fv"], VE_done_for_xrb)
                    SY.wait(C["fs"], C["fs"].n)
                    SY.wait(C["dy"], C["dy"].n)
                    SY.op("dma_start", out=xrb[0][:, :],
                          in_=x_in[tt * 128:(tt + 1) * 128, :],
                          inc=(C["dxr"], 16))
                    ndsx = C["dxr"].n
                    for es_ in range(4):
                        G = emit_Y_tile.att_G
                        apar = G % 2
                        TE.wait(C["doutT"], n_doutT)
                        TE.wait(C["dwo"], 16)
                        if G >= 2:
                            TE.wait(C["sat"], G - 1)
                        for fc in range(16):
                            TE.op("matmul", ps_at[apar][:, :],
                                  outT[:, fc * 512 + tl * 128:
                                       fc * 512 + (tl + 1) * 128],
                                  wob[:, fc * D + es_ * 512:
                                      fc * D + (es_ + 1) * 512],
                                  start=(fc == 0), stop=(fc == 15),
                                  inc=(C["mat"], 1) if fc == 15 else None)
                        SC.wait(C["mat"], C["mat"].n)
                        if es_ == 0 and tt >= 2:
                            SC.wait(C["fv"], vy_hist[tt - 2])
                        SC.op("activation",
                              attnf[tt % 2][:, es_ * 512:(es_ + 1) * 512],
                              ps_at[apar][:, :], AF.Copy, bias=0.0,
                              scale=np.float32(HD_SCALE), inc=(C["sat"], 1))
                        emit_Y_tile.att_G += 1
                    VE.wait(C["sat"], C["sat"].n)
                    ln1 = ln_stats_apply(attnf[tt % 2][:, :], attnf[tt % 2][:, :])
                    ln_hist[tt] = ln1
                    VE.wait(C["fs"], ln1)
                    VE.wait(C["dxr"], ndsx)
                    vy = VE.op("tensor_add", xrb[0][:, :], xrb[0][:, :],
                               attnf[tt % 2][:, :], inc=(C["fv"], 1))
                    vy_hist[tt] = vy
                    SY.wait(C["fv"], vy)
                    SY.op("dma_start", out=y_st[tt * 128:(tt + 1) * 128, :],
                          in_=xrb[0][:, :], inc=(C["dy"], 16))
                    VE.wait(C["fv"], vy)
                    SC.wait(C["fv"], vy)
                    SC.wait(C["dyn"], C["dyn"].n)
                    ln2 = ln_stats_apply(xrb[0][:, :], ynb[:, :])
                    SY.wait(C["fs"], ln2)
                    SY.op("dma_start", out=yn_st[tt * 128:(tt + 1) * 128, :],
                          in_=ynb[:, :], inc=(C["dyn"], 16))
            emit_Y_tile.att_G = 0

            for tl in range(4):
                emit_X_tile(0, tl)
            for ch in range(NCH):
                emit_Y_head(ch)
                for tl in range(4):
                    if ch + 1 < NCH:
                        emit_X_tile(ch + 1, tl)
                    emit_Y_tile(ch, tl)
            n_dy = C["dy"].n
            n_dyn = C["dyn"].n
            n_fv_2a = C["fv"].n
            n_sat = C["sat"].n
            assert n_dy == 16 * NTT and n_dyn == 16 * NTT

        phase_barrier(["dsr0", "dsr1", "dsr2", "dsg0", "dsg1", "dsg2",
                       "dcr", "dxr", "dout0", "dout1",
                       "doutT", "dy", "dyn", "dwo", "mcs",
                       "mat", "sat", "vat", "fs", "fv"])

        with contextlib.ExitStack() as p2b:
            ynT = p2b.enter_context(nc.sbuf_tensor("ynT", [128, 16 * 512], BF16))
            wgb = [p2b.enter_context(
                nc.sbuf_tensor(f"wgb{i}", [128, 16 * 256], BF16)) for i in range(2)]
            wub = [p2b.enter_context(
                nc.sbuf_tensor(f"wub{i}", [128, 16 * 256], BF16)) for i in range(2)]
            wdb = [p2b.enter_context(
                nc.sbuf_tensor(f"wdb{i}", [128, 4 * 1024], BF16)) for i in range(2)]
            fuse = p2b.enter_context(nc.sbuf_tensor("fuse", [128, 64 * 512], BF16))
            silub = [p2b.enter_context(
                nc.sbuf_tensor(f"silub{i}", [128, 512], F32)) for i in range(2)]
            sdb = p2b.enter_context(nc.sbuf_tensor("sdb", [128, D], F32))
            zf = [p2b.enter_context(nc.sbuf_tensor(f"zf{i}", [128, D], F32))
                  for i in range(4)]
            zlnb = p2b.enter_context(nc.sbuf_tensor("zlnb", [128, D], F32))
            yrb = p2b.enter_context(nc.sbuf_tensor("yrb", [128, D], F32))
            sd_sb = p2b.enter_context(nc.sbuf_tensor("sd_sb", [1, D], F32))
            SY.op("dma_start", out=sd_sb[:, :], in_=sd_in, inc=(C["dsd"], 16))

            TE.wait(C["fv"], n_fv_2a)
            TE.wait(C["sat"], n_sat)
            TE.wait(C["dsd"], 16)
            with nc.psum_tensor("ps_sd", [128, D], F32) as ps_sd:
                for sl in range(4):
                    TE.op("matmul", ps_sd[:, sl * 512:(sl + 1) * 512],
                          ones1_sb[0:1, :], sd_sb[0:1, sl * 512:(sl + 1) * 512],
                          start=True, stop=True,
                          inc=(C["mts"], 1) if sl == 3 else None)
                VE.wait(C["mts"], C["mts"].n)
                VE.op("tensor_copy", sdb[:, :], ps_sd[:, :], inc=(C["vz"], 1))

            mg_hist = []
            g_glob = 0
            ocg_glob = 0
            eh_glob = 0

            def emit_ynT_transposes(ch):
                SC.wait(C["dyn"], n_dyn)
                SC.wait(C["mg"], C["mg"].n)
                for fc in range(16):
                    SC.op("dma_start_transpose", ynT[:, fc * 512:(fc + 1) * 512],
                          yn_st[ch * 512:(ch + 1) * 512,
                                fc * 128:(fc + 1) * 128],
                          inc=(C["dynT"], 16))
                return C["dynT"].n

            dynT_hist = [emit_ynT_transposes(0)]
            for ch in range(NCH):
                n_dynT = dynT_hist[ch]

                stA = contextlib.ExitStack()
                ps_g = [stA.enter_context(
                    nc.psum_tensor(f"ps_g{ch}_{i}", [128, 512], F32))
                    for i in range(2)]
                ps_u = [stA.enter_context(
                    nc.psum_tensor(f"ps_u{ch}_{i}", [128, 512], F32))
                    for i in range(2)]
                if ch > 0:
                    VE.wait(C["mzb"], 32 * ch)
                TE.wait(C["vz"], C["vz"].n)
                for om in range(32):
                    om_glob = ch * 32 + om
                    par = om % 2
                    if om_glob >= 2:
                        GP.wait(C["mg"], mg_hist[om_glob - 2])
                    srcg = wg_in.rearrange("(dc p) o -> p dc o", p=128)[
                        :, :, om * 256:(om + 1) * 256]
                    srcu = wu_in.rearrange("(dc p) o -> p dc o", p=128)[
                        :, :, om * 256:(om + 1) * 256]
                    dwgp = C["dwg0"] if par == 0 else C["dwg1"]
                    GP.op("dma_start",
                          out=wgb[par][:, :].rearrange("p (dc o) -> p dc o", o=256),
                          in_=srcg, inc=(dwgp, 16))
                    GP.op("dma_start",
                          out=wub[par][:, :].rearrange("p (dc o) -> p dc o", o=256),
                          in_=srcu, inc=(dwgp, 16))
                    ndw = dwgp.n
                    for oc4 in range(2):
                        go = om * 2 + oc4
                        gpar = g_glob % 2
                        TE.wait(dwgp, ndw)
                        TE.wait(C["dynT"], n_dynT)
                        if g_glob >= 2:
                            TE.wait(C["sfa"], g_glob - 1)
                            TE.wait(C["vfa"], g_glob - 1)
                        for dc in range(16):
                            TE.op("matmul", ps_g[gpar][:, :],
                                  wgb[par][:, dc * 256 + oc4 * 128:
                                           dc * 256 + (oc4 + 1) * 128],
                                  ynT[:, dc * 512:(dc + 1) * 512],
                                  start=(dc == 0), stop=(dc == 15))
                        for dc in range(16):
                            TE.op("matmul", ps_u[gpar][:, :],
                                  wub[par][:, dc * 256 + oc4 * 128:
                                           dc * 256 + (oc4 + 1) * 128],
                                  ynT[:, dc * 512:(dc + 1) * 512],
                                  start=(dc == 0), stop=(dc == 15),
                                  inc=(C["mg"], 1) if dc == 15 else None)
                        mgn = C["mg"].n
                        SC.wait(C["mg"], mgn)
                        if g_glob >= 2:
                            SC.wait(C["vfa"], g_glob - 1)
                        sa = SC.op("activation", silub[gpar][:, :],
                                   ps_g[gpar][:, :], AF.Silu,
                                   scale=sg_sb[:, go:go + 1], inc=(C["sfa"], 1))
                        VE.wait(C["sfa"], sa)
                        VE.op("scalar_tensor_tensor",
                              fuse[:, go * 512:(go + 1) * 512],
                              ps_u[gpar][:, :], su_sb[:, go:go + 1],
                              silub[gpar][:, :], ALU.mult, ALU.mult,
                              inc=(C["vfa"], 1))
                        g_glob += 1
                    mg_hist.append(C["mg"].n)
                nvfa = C["vfa"].n
                if ch + 1 < NCH:
                    dynT_hist.append(emit_ynT_transposes(ch + 1))
                stA.close()
                stB = contextlib.ExitStack()
                ps_z = [stB.enter_context(
                    nc.psum_tensor(f"ps_z{ch}_{i}", [128, 1024], F32))
                    for i in range(4)]

                for eh in range(2):
                    for ocg in range(16):
                        par = ocg_glob % 2
                        if ocg_glob >= 2:
                            GP.wait(C["mzb"], ocg_glob - 1)
                        srcd = wd_in.rearrange("(g p) e -> p g e", p=128)[
                            :, ocg * 4:(ocg + 1) * 4,
                            eh * 1024:(eh + 1) * 1024]
                        dwdp = C["dwd0"] if par == 0 else C["dwd1"]
                        GP.op("dma_start",
                              out=wdb[par][:, :].rearrange(
                                  "p (g e) -> p g e", e=1024),
                              in_=srcd, inc=(dwdp, 16))
                        ndd = dwdp.n
                        TE.wait(dwdp, ndd)
                        TE.wait(C["vfa"], nvfa)
                        if ocg == 0:
                            TE.wait(C["vz"], C["vz"].n)
                        for oc4 in range(4):
                            oc = ocg * 4 + oc4
                            for tl in range(4):
                                for es_ in range(2):
                                    last = (oc4 == 3 and tl == 3 and es_ == 1)
                                    TE.op("matmul",
                                          ps_z[tl][:, es_ * 512:(es_ + 1) * 512],
                                          fuse[:, oc * 512 + tl * 128:
                                               oc * 512 + (tl + 1) * 128],
                                          wdb[par][:, oc4 * 1024 + es_ * 512:
                                                   oc4 * 1024 + (es_ + 1) * 512],
                                          start=(oc == 0), stop=(oc == 63),
                                          inc=(C["mzb"], 1) if last else None)
                        ocg_glob += 1
                    mzn = C["mzb"].n
                    for tl in range(4):
                        VE.wait(C["mzb"], mzn)
                        if eh == 0 and ch > 0:
                            VE.wait(C["fs"], C["fs"].n)
                        VE.op("scalar_tensor_tensor",
                              zf[tl][:, eh * 1024:(eh + 1) * 1024],
                              ps_z[tl][:, :],
                              1.0, sdb[:, eh * 1024:(eh + 1) * 1024],
                              ALU.mult, ALU.mult, inc=(C["vz"], 1))
                    eh_glob += 1
                stB.close()

                for tl in range(4):
                    tt = ch * 4 + tl
                    SY.wait(C["fv"], C["fv"].n)
                    SY.wait(C["dfin"], C["dfin"].n)
                    SY.op("dma_start", out=yrb[:, :],
                          in_=y_st[tt * 128:(tt + 1) * 128, :], inc=(C["dyr"], 16))
                    ndsy = C["dyr"].n
                    VE.wait(C["vz"], C["vz"].n)
                    SC.wait(C["fv"], C["fv"].n)
                    ln3 = ln_stats_apply(zf[tl][:, :], zlnb[:, :])
                    VE.wait(C["fs"], ln3)
                    VE.wait(C["dyr"], ndsy)
                    vf = VE.op("tensor_add", yrb[:, :], yrb[:, :], zlnb[:, :],
                               inc=(C["fv"], 1))
                    SY.wait(C["fv"], vf)
                    SY.op("dma_start", out=y_out[tt * 128:(tt + 1) * 128, :],
                          in_=yrb[:, :], inc=(C["dfin"], 16))
            assert C["dfin"].n == 16 * NTT

        SY.wait(C["dfin"], 16 * NTT)

        with nc.Block() as block:
            @block.sync
            def _(e):
                recs["sync"].replay(e)

            @block.tensor
            def _(e):
                recs["tensor"].replay(e)

            @block.scalar
            def _(e):
                recs["scalar"].replay(e)

            @block.vector
            def _(e):
                recs["vector"].replay(e)

            @block.gpsimd
            def _(e):
                recs["gpsimd"].replay(e)

    return nc


def _host_consts(T):
    import ml_dtypes
    u = np.triu(np.ones((128, 128))).astype(ml_dtypes.bfloat16)
    ones1f = np.ones((1, 128), dtype=np.float32)
    return u, ones1f


def _make_in_maps(x, wqkvT, woutT, wg, wu, wd, sg_c, su_c, sd_row, T):
    u, ones1f = _host_consts(T)
    NTT = T // 128
    SG = NCORES * NTT
    in_maps = []
    for c in range(NCORES):
        gmap = np.empty(SG, dtype=np.int64)
        gmap[0:NTT] = NTT * c + np.arange(NTT)
        fb = [b for b in range(NCORES) if b != c]
        for i, b in enumerate(fb):
            gmap[NTT + i * NTT: NTT + (i + 1) * NTT] = NTT * b + np.arange(NTT)
        ps_sel = (gmap[:, None] < (NTT * c + np.arange(NTT))[None, :]
                  ).astype(np.float32)
        x_pre = np.ascontiguousarray(
            np.concatenate([x[b * T:(b + 1) * T] for b in fb], axis=0))
        in_maps.append(dict(
            x_sh=np.ascontiguousarray(x[c * T:(c + 1) * T]),
            x_full=x_pre,
            wqkvT=wqkvT, woutT=woutT, wg=wg, wu=wu, wd=wd,
            sg_c=sg_c, su_c=su_c, sd_row=sd_row,
            u_tri=u, ones1f=ones1f, ps_sel=ps_sel,
        ))
    return in_maps


def kernel(L0_attn_x, L0_attn_W_qkv, L0_attn_W_out, L0_ffn_w_gate, L0_ffn_w_up,
           L0_ffn_w_down, L0_ffn_s_gate, L0_ffn_s_up, L0_ffn_s_down):
    T = 16384 // NCORES
    x = np.asarray(L0_attn_x).reshape(16384, D)
    wqkvT = np.ascontiguousarray(np.asarray(L0_attn_W_qkv).T)
    woutT = np.ascontiguousarray(np.asarray(L0_attn_W_out).T)
    wg = np.ascontiguousarray(np.asarray(L0_ffn_w_gate))
    wu = np.ascontiguousarray(np.asarray(L0_ffn_w_up))
    wd = np.ascontiguousarray(np.asarray(L0_ffn_w_down))
    sg_c = np.ascontiguousarray(
        np.asarray(L0_ffn_s_gate).reshape(DI // 128, 128).T).astype(np.float32)
    su_c = np.ascontiguousarray(
        np.asarray(L0_ffn_s_up).reshape(DI // 128, 128).T).astype(np.float32)
    sd_row = np.ascontiguousarray(
        np.asarray(L0_ffn_s_down).reshape(1, D)).astype(np.float32)

    nc = build_nc(T)
    in_maps = _make_in_maps(x, wqkvT, woutT, wg, wu, wd, sg_c, su_c, sd_row, T)
    res = run_bass_kernel_spmd(nc, in_maps, core_ids=list(range(NCORES)))
    out = np.concatenate([res.results[c]["y_out"] for c in range(NCORES)], axis=0)
    return out.reshape(1, 16384, D).astype(np.float32)


# version 20 (final)
# speedup vs baseline: 1.1618x; 1.1618x over previous
import contextlib
import numpy as np
import concourse.bass as bass
import concourse.mybir as mybir
from concourse.bass_utils import run_bass_kernel_spmd

F32 = mybir.dt.float32
BF16 = mybir.dt.bfloat16
I8 = mybir.dt.int8
AF = mybir.ActivationFunctionType
ALU = mybir.AluOpType
AX = mybir.AxisListType

D = 2048
DI = 8192
NCORES = 8
LN_EPS = 1e-5
ATT_EPS = 1e-9
HD_SCALE = float(128.0 ** -0.5)


class Ctr:
    def __init__(self, sem):
        self.sem = sem
        self.n = 0

    def bump(self, k):
        self.n += k
        return self.n


class Rec:
    def __init__(self, name):
        self.name = name
        self.items = []

    def wait(self, ctr, val):
        assert 0 <= val <= ctr.n, \
            f"{self.name} waits {val} but ctr max-so-far {ctr.n}"
        if val == 0:
            return
        sem = ctr.sem
        self.items.append(lambda e: e.wait_ge(sem, val))

    def op(self, meth, *args, inc=None, **kw):
        after = None
        sem = val = None
        if inc is not None:
            ctr, amt = inc
            after = ctr.bump(amt)
            sem, val = ctr.sem, amt

        def emit(e, meth=meth, args=args, kw=kw, sem=sem, val=val):
            ins = getattr(e, meth)(*args, **kw)
            if sem is not None:
                ins.then_inc(sem, val)
        self.items.append(emit)
        return after

    def replay(self, e):
        for f in self.items:
            f(e)


def build_nc(T):
    NTT = T // 128
    NCH = T // 512
    assert T % 512 == 0

    nc = bass.Bass("TRN2", target_bir_lowering=False, debug=False)

    x_in = nc.dram_tensor("x_sh", [T, D], F32, kind="ExternalInput").ap()
    wqkvT = nc.dram_tensor("wqkvT", [D, 3 * D], F32, kind="ExternalInput").ap()
    woutT = nc.dram_tensor("woutT", [D, D], F32, kind="ExternalInput").ap()
    wg_in = nc.dram_tensor("wg", [D, DI], I8, kind="ExternalInput").ap()
    wu_in = nc.dram_tensor("wu", [D, DI], I8, kind="ExternalInput").ap()
    wd_in = nc.dram_tensor("wd", [DI, D], I8, kind="ExternalInput").ap()
    sg_in = nc.dram_tensor("sg_c", [128, DI // 128], F32, kind="ExternalInput").ap()
    su_in = nc.dram_tensor("su_c", [128, DI // 128], F32, kind="ExternalInput").ap()
    sd_in = nc.dram_tensor("sd_row", [1, D], F32, kind="ExternalInput").ap()
    u_in = nc.dram_tensor("u_tri", [128, 128], BF16, kind="ExternalInput").ap()
    ones1_in = nc.dram_tensor("ones1f", [1, 128], F32, kind="ExternalInput").ap()
    SG = NCORES * NTT
    x_full = nc.dram_tensor("x_full", [(NCORES - 1) * T, D], F32,
                            kind="ExternalInput").ap()
    ps_in = nc.dram_tensor("ps_sel", [SG, NTT], F32, kind="ExternalInput").ap()

    y_out = nc.dram_tensor("y_out", [T, D], F32, kind="ExternalOutput").ap()

    xn_st = nc.dram_tensor("xn_st", [16, T, 128], BF16).ap()
    xn_st2 = nc.dram_tensor("xn_st2", [16, T, 128], BF16).ap()
    q_st = nc.dram_tensor("q_st", [T, D], BF16).ap()
    k_st = nc.dram_tensor("k_st", [T, D], BF16).ap()
    kv_st = nc.dram_tensor("kv_st", [T, D], BF16).ap()
    ts_st = nc.dram_tensor("ts_st", [2, NCORES * NTT, D], F32).ap()
    car_st = nc.dram_tensor("car_st", [2, NTT, D], F32).ap()
    car_hi = nc.dram_tensor("car_hi", [2, NTT, D], BF16).ap()
    car_lo = nc.dram_tensor("car_lo", [2, NTT, D], BF16).ap()
    out_st = nc.dram_tensor("out_st", [T, D], BF16).ap()
    y_st = nc.dram_tensor("y_st", [T, D], F32).ap()
    yn_st = nc.dram_tensor("yn_st", [T, D], BF16).ap()

    recs = {k: Rec(k) for k in ("sync", "tensor", "scalar", "vector", "gpsimd")}
    SY, TE, SC, VE, GP = (recs[k] for k in
                          ("sync", "tensor", "scalar", "vector", "gpsimd"))

    es = contextlib.ExitStack()
    with es:
        sem_names = ["dx0", "dx1", "dxn0", "dxn1", "dxnT", "dw0", "dw1",
                     "dstK", "dstKV", "dstQ", "dts0", "dts1", "dcar",
                     "dsr0", "dsr1", "dcr", "dxr", "dyr",
                     "dout0", "dout1", "doutT", "dy", "dyn", "dynT",
                     "dwg0", "dwg1", "dwd0", "dwd1", "dfin",
                     "mqkv", "mts", "mev", "mtse", "v15",
                     "mcs", "fv", "fs", "mat", "sat", "vat", "mg", "sfa", "vfa",
                     "mzb", "vz", "dmisc", "dwo", "fg", "scv", "dsg0", "dsg1", "dsr2", "dsg2", "dsd"]
        C = {}
        for s in sem_names:
            C[s] = Ctr(es.enter_context(nc.semaphore(s)))

        def phase_barrier(ctrs):
            for R in (SY, TE, SC, VE, GP):
                for cname in ctrs:
                    R.wait(C[cname], C[cname].n)

        u_sb = es.enter_context(nc.sbuf_tensor("u_sb", [128, 128], BF16))
        ps_sb = es.enter_context(nc.sbuf_tensor("ps_sb", [SG, NTT], F32))
        ones1_sb = es.enter_context(nc.sbuf_tensor("ones1_sb", [1, 128], F32))
        sg_sb = es.enter_context(nc.sbuf_tensor("sg_sb", [128, DI // 128], F32))
        su_sb = es.enter_context(nc.sbuf_tensor("su_sb", [128, DI // 128], F32))
        for dst, src in [(u_sb, u_in), (ps_sb, ps_in),
                         (ones1_sb, ones1_in), (sg_sb, sg_in), (su_sb, su_in)]:
            SY.op("dma_start", out=dst[:, :], in_=src, inc=(C["dmisc"], 16))
        n_const = C["dmisc"].n
        for R in (TE, SC, VE, GP):
            R.wait(C["dmisc"], n_const)

        lnt = {}
        for nmx in ("s1", "s2", "mean", "v2", "m2", "var", "sd", "rstd", "nmr"):
            lnt[nmx] = es.enter_context(nc.sbuf_tensor("ln_" + nmx, [128, 1], F32))
        sq_scr = es.enter_context(nc.sbuf_tensor("sq_scr", [128, D], BF16))

        def ln_stats_apply(x_ap, out_ap):
            a = VE.op("reduce_sum", lnt["s1"][:, 0:1], x_ap, axis=AX.X,
                      inc=(C["fv"], 1))
            b = VE.op("scalar_tensor_tensor", sq_scr[:, :], x_ap, 1.0, x_ap,
                      ALU.mult, ALU.mult, accum_out=lnt["s2"][:, 0:1],
                      inc=(C["fv"], 1))
            VE.wait(C["fv"], a)
            c = VE.op("tensor_scalar_mul", lnt["mean"][:, 0:1], lnt["s1"][:, 0:1],
                      1.0 / D, inc=(C["fv"], 1))
            VE.wait(C["fv"], b)
            d = VE.op("tensor_scalar_mul", lnt["v2"][:, 0:1], lnt["s2"][:, 0:1],
                      1.0 / D, inc=(C["fv"], 1))
            VE.wait(C["fv"], c)
            e = VE.op("tensor_mul", lnt["m2"][:, 0:1], lnt["mean"][:, 0:1],
                      lnt["mean"][:, 0:1], inc=(C["fv"], 1))
            VE.wait(C["fv"], e)
            f = VE.op("tensor_sub", lnt["var"][:, 0:1], lnt["v2"][:, 0:1],
                      lnt["m2"][:, 0:1], inc=(C["fv"], 1))
            SC.wait(C["fv"], f)
            g = SC.op("activation", lnt["sd"][:, 0:1], lnt["var"][:, 0:1], AF.Ln,
                      bias=np.float32(LN_EPS), inc=(C["fs"], 1))
            SC.wait(C["fs"], g)
            h = SC.op("activation", lnt["rstd"][:, 0:1], lnt["sd"][:, 0:1],
                      AF.Exp, scale=np.float32(-0.5), inc=(C["fs"], 1))
            VE.wait(C["fs"], h)
            i = VE.op("scalar_tensor_tensor", lnt["nmr"][:, 0:1],
                      lnt["mean"][:, 0:1], -1.0, lnt["rstd"][:, 0:1],
                      ALU.mult, ALU.mult, inc=(C["fv"], 1))
            SC.wait(C["fv"], i)
            j = SC.op("activation", out_ap, x_ap, AF.Identity,
                      scale=lnt["rstd"][:, 0:1], bias=lnt["nmr"][:, 0:1],
                      inc=(C["fs"], 1))
            return j


        def p0_load(x_src, row0, tt, xbuf, ln_done, dxp2, dxnp2):
            par = tt % 2
            dxp = dxp2[par]
            if ln_done[par] > 0:
                SY.wait(C["fs"], ln_done[par])
                SY.wait(dxnp2[par], dxnp2[par].n)
            SY.op("dma_start", out=xbuf[par][:, :],
                  in_=x_src[row0 + tt * 128:row0 + (tt + 1) * 128, :],
                  inc=(dxp, 16))
            return dxp.n

        def p0_stats(tt, xbuf, dxp2, nload):
            par = tt % 2
            VE.wait(dxp2[par], nload)
            a = VE.op("reduce_sum", lnt["s1"][:, 0:1], xbuf[par][:, :],
                      axis=AX.X, inc=(C["fv"], 1))
            b = VE.op("scalar_tensor_tensor", sq_scr[:, :], xbuf[par][:, :],
                      1.0, xbuf[par][:, :], ALU.mult, ALU.mult,
                      accum_out=lnt["s2"][:, 0:1], inc=(C["fv"], 1))
            return a, b

        def p0_chain(ab):
            a, b = ab
            VE.wait(C["fv"], a)
            c = VE.op("tensor_scalar_mul", lnt["mean"][:, 0:1], lnt["s1"][:, 0:1],
                      1.0 / D, inc=(C["fv"], 1))
            VE.wait(C["fv"], b)
            d = VE.op("tensor_scalar_mul", lnt["v2"][:, 0:1], lnt["s2"][:, 0:1],
                      1.0 / D, inc=(C["fv"], 1))
            VE.wait(C["fv"], c)
            e = VE.op("tensor_mul", lnt["m2"][:, 0:1], lnt["mean"][:, 0:1],
                      lnt["mean"][:, 0:1], inc=(C["fv"], 1))
            VE.wait(C["fv"], e)
            f = VE.op("tensor_sub", lnt["var"][:, 0:1], lnt["v2"][:, 0:1],
                      lnt["m2"][:, 0:1], inc=(C["fv"], 1))
            SC.wait(C["fv"], f)
            g = SC.op("activation", lnt["sd"][:, 0:1], lnt["var"][:, 0:1], AF.Ln,
                      bias=np.float32(LN_EPS), inc=(C["fs"], 1))
            SC.wait(C["fs"], g)
            h = SC.op("activation", lnt["rstd"][:, 0:1], lnt["sd"][:, 0:1],
                      AF.Exp, scale=np.float32(-0.5), inc=(C["fs"], 1))
            return h

        def p0_apply(tt, dst_st, xbuf, xnb, ln_done, dxp2, dxnp2, h):
            par = tt % 2
            VE.wait(C["fs"], h)
            i = VE.op("scalar_tensor_tensor", lnt["nmr"][:, 0:1],
                      lnt["mean"][:, 0:1], -1.0, lnt["rstd"][:, 0:1],
                      ALU.mult, ALU.mult, inc=(C["fv"], 1))
            SC.wait(C["fv"], i)
            j = SC.op("activation", xnb[par][:, :], xbuf[par][:, :], AF.Identity,
                      scale=lnt["rstd"][:, 0:1], bias=lnt["nmr"][:, 0:1],
                      inc=(C["fs"], 1))
            ln_done[par] = j
            SY.wait(C["fs"], j)
            SY.op("dma_start",
                  out=dst_st[:, tt * 128:(tt + 1) * 128, :].rearrange(
                      "dc p d -> p dc d"),
                  in_=xnb[par][:, :].rearrange("p (dc d) -> p dc d", d=128),
                  inc=(dxnp2[par], 16))

        def p0_compute(tt, dst_st, xbuf, xnb, ln_done, dxp2, dxnp2, nload):
            ab = p0_stats(tt, xbuf, dxp2, nload)
            h = p0_chain(ab)
            p0_apply(tt, dst_st, xbuf, xnb, ln_done, dxp2, dxnp2, h)

        def p0_tile(x_src, row0, tt, dst_st, xbuf, xnb, ln_done, dxp2, dxnp2):
            n = p0_load(x_src, row0, tt, xbuf, ln_done, dxp2, dxnp2)
            p0_compute(tt, dst_st, xbuf, xnb, ln_done, dxp2, dxnp2, n)

        def emit_p0_block(x_src, row0, dst_st, xbuf, xnb, ln_done, dxp2, dxnp2):
            for tt in range(NTT):
                p0_tile(x_src, row0, tt, dst_st, xbuf, xnb, ln_done, dxp2, dxnp2)

        class P1State:
            def __init__(self):
                self.gidx = 0
                self.ts_group = 0
                self.pending_ts = []

        mev_hist = {}

        def emit_qkv_pass(st, jt_list, xnT, wqb, kbuf, kvbuf, qbuf, r2b,
                          tsrow, ps_qkv, ps_ts, do_stash, ts_base, bg=None,
                          tail_hook=None):
            for ji0, (kind, jt) in enumerate(jt_list):
                js = jt * 512
                col = js - {"k": 2048, "v": 4096, "q": 0}[kind]
                dwp = C["dw0"] if st.gidx // NTT % 2 == 0 else C["dw1"]
                par_w = (st.gidx // NTT) % 2
                if st.gidx // NTT >= 2:
                    GP.wait(C["mqkv"], st.gidx - NTT)
                srcw = wqkvT.rearrange("(dc p) j -> p dc j", p=128)[
                    :, :, js:js + 512]
                GP.op("dma_start",
                      out=wqb[par_w][:, :].rearrange("p (dc j) -> p dc j", j=512),
                      in_=srcw, inc=(dwp, 16))

                if kind == "k":
                    VE.wait(C["dstK"], C["dstK"].n)
                    VE.wait(C["mts"], C["mts"].n)
                if kind == "v":
                    VE.wait(C["dstKV"], C["dstKV"].n)
                    VE.wait(C["mts"], C["mts"].n)
                if kind == "q":
                    VE.wait(C["dstQ"], C["dstQ"].n)

                for tt in range(NTT):
                    par = st.gidx % 4
                    TE.wait(dwp, dwp.n)
                    TE.wait(C["dxnT"], C["dxnT"].n)
                    if st.gidx >= 4:
                        TE.wait(C["mev"], st.gidx - 3)
                    for dc in range(16):
                        TE.op("matmul", ps_qkv[par][:, :],
                              xnT[:, dc * T + tt * 128: dc * T + (tt + 1) * 128],
                              wqb[par_w][:, dc * 512:(dc + 1) * 512],
                              start=(dc == 0), stop=(dc == 15),
                              inc=(C["mqkv"], 1) if dc == 15 else None)
                    mq_now = C["mqkv"].n

                    if kind in ("k", "q"):
                        SC.wait(C["mqkv"], mq_now)
                        if st.gidx >= 4:
                            SC.wait(C["mev"], st.gidx - 3)
                        a = SC.op("activation", r2b[par][:, :], ps_qkv[par][:, :],
                                  AF.Relu, scale=-1.0, inc=(C["fs"], 1))
                        SC.wait(C["fs"], a)
                        b = SC.op("activation", r2b[par][:, :], r2b[par][:, :],
                                  AF.Exp, scale=-1.0, inc=(C["fs"], 1))
                        VE.wait(C["fs"], b)
                        buf = kbuf if kind == "k" else qbuf
                        VE.op("scalar_tensor_tensor",
                              buf[:, tt * 512:(tt + 1) * 512],
                              ps_qkv[par][:, :], 0.0, r2b[par][:, :],
                              ALU.max, ALU.add, inc=(C["mev"], 1))
                    else:
                        VE.wait(C["mqkv"], mq_now)
                        VE.wait(C["mev"], st.gidx - NTT + 1)
                        VE.op("scalar_tensor_tensor",
                              kvbuf[:, tt * 512:(tt + 1) * 512],
                              ps_qkv[par][:, :], 1.0,
                              kbuf[:, tt * 512:(tt + 1) * 512],
                              ALU.mult, ALU.mult, inc=(C["mev"], 1))
                    mev_now = C["mev"].n

                    mev_hist[st.gidx] = mev_now
                    st.gidx += 1
                    if st.pending_ts and st.gidx % 2 == 1:
                        st.pending_ts.pop(0)()
                    if bg and st.gidx % 2 == 0:
                        bg.pop(0)()

                if tail_hook is not None and ji0 == len(jt_list) - 1:
                    tail_hook()
                    tail_hook = None
                if ts_base is not None and kind in ("k", "v"):
                    tbuf = kbuf if kind == "k" else kvbuf
                    for t0 in range(0, NTT, 2):
                        def ts_group_emit(t0=t0, tbuf=tbuf, kind=kind, col=col,
                                          base=ts_base,
                                          mevs=(mev_hist[st.gidx - NTT + t0],
                                                mev_hist[st.gidx - NTT + t0 + 1])):
                            gpar = st.ts_group % 2
                            for j, tt in enumerate((t0, t0 + 1)):
                                TE.wait(C["mev"], mevs[j])
                                if j == 0 and st.ts_group >= 2:
                                    TE.wait(C["mtse"], st.ts_group - 1)
                                TE.op("matmul",
                                      ps_ts[gpar][0:1, j * 512:j * 512 + 512],
                                      u_sb[:, 127:128],
                                      tbuf[:, tt * 512:(tt + 1) * 512],
                                      start=True, stop=True, inc=(C["mts"], 1))
                            dtsp = C["dts0"] if gpar == 0 else C["dts1"]
                            SC.wait(C["mts"], C["mts"].n)
                            SC.wait(dtsp, dtsp.n)
                            ev = SC.op("copy", tsrow[gpar][0:1, :],
                                       ps_ts[gpar][0:1, :], inc=(C["mtse"], 1))
                            st.ts_group += 1
                            SY.wait(C["mtse"], ev)
                            idx = 0 if kind == "k" else 1
                            SY.op("dma_start",
                                  out=ts_st[idx:idx + 1, base + t0:base + t0 + 2,
                                            col:col + 512],
                                  in_=tsrow[gpar][0:1, :].rearrange(
                                      "p (g j) -> p g j", j=512),
                                  inc=(dtsp, 16))
                        st.pending_ts.append(ts_group_emit)

                if ji0 == len(jt_list) - 1:
                    while st.pending_ts:
                        st.pending_ts.pop(0)()
                if do_stash:
                    buf = {"k": kbuf, "v": kvbuf, "q": qbuf}[kind]
                    stg = {"k": k_st, "v": kv_st, "q": q_st}[kind]
                    SY.wait(C["mev"], C["mev"].n)
                    stsem = {"k": C["dstK"], "v": C["dstKV"],
                             "q": C["dstQ"]}[kind]
                    SY.op("dma_start",
                          out=stg.rearrange("(tt p) c -> p tt c", p=128)[
                              :, :, col:col + 512],
                          in_=buf[:, :].rearrange("p (tt j) -> p tt j", j=512),
                          inc=(stsem, 16))

        with contextlib.ExitStack() as p1:
            xbuf = [p1.enter_context(nc.sbuf_tensor(f"xb{i}", [128, D], F32))
                    for i in range(2)]
            xnb = [p1.enter_context(nc.sbuf_tensor(f"xnb{i}", [128, D], BF16))
                   for i in range(2)]
            xnT = p1.enter_context(nc.sbuf_tensor("xnT", [128, 16 * T], BF16))
            wqb = [p1.enter_context(
                nc.sbuf_tensor(f"wqb{i}", [128, 16 * 512], BF16))
                for i in range(2)]
            kbuf = p1.enter_context(nc.sbuf_tensor("kbuf", [128, NTT * 512], BF16))
            kvbuf = p1.enter_context(
                nc.sbuf_tensor("kvbuf", [128, NTT * 512], BF16))
            qbuf = p1.enter_context(nc.sbuf_tensor("qbuf", [128, NTT * 512], BF16))
            r2b = [p1.enter_context(nc.sbuf_tensor(f"r2b{i}", [128, 512], F32))
                   for i in range(4)]
            tsrow = [p1.enter_context(nc.sbuf_tensor(f"tsrow{i}", [1, 1024], F32))
                     for i in range(2)]
            ps_qkv = [p1.enter_context(nc.psum_tensor(f"psq{i}", [128, 512], F32))
                      for i in range(4)]
            ps_ts = [p1.enter_context(nc.psum_tensor(f"psts{i}", [1, 1024], F32))
                     for i in range(2)]

            st = P1State()
            ln_done = [0, 0]
            dxp2 = [C["dx0"], C["dx1"]]
            dxnp2 = [C["dxn0"], C["dxn1"]]

            own_jts = []
            for i in range(4):
                own_jts.append(("k", 4 + i))
                own_jts.append(("v", 8 + i))
            for i in range(4):
                own_jts.append(("q", i))
            full_jts = []
            for i in range(4):
                full_jts.append(("k", 4 + i))
                full_jts.append(("v", 8 + i))

            xst = [xn_st, xn_st2]

            def mk_bg(blk):
                items = []
                ndxnT = C["dxnT"].n
                state = {}

                def f_stats(k):
                    if k == 0:
                        SY.wait(C["dxnT"], ndxnT)
                        state[("ld", 0)] = p0_load(x_full, blk * T, 0, xbuf,
                                                   ln_done, dxp2, dxnp2)
                    if k + 1 < NTT:
                        state[("ld", k + 1)] = p0_load(x_full, blk * T, k + 1,
                                                       xbuf, ln_done, dxp2,
                                                       dxnp2)
                    state[("ab", k)] = p0_stats(k, xbuf, dxp2,
                                                state[("ld", k)])

                def f_chain(k):
                    state[("h", k)] = p0_chain(state[("ab", k)])

                def f_apply(k):
                    p0_apply(k, xst[blk % 2], xbuf, xnb, ln_done, dxp2, dxnp2,
                             state[("h", k)])

                for tt in range(NTT):
                    items.append(lambda k=tt: f_stats(k))
                    items.append(lambda k=tt: f_chain(k))
                    items.append(lambda k=tt: f_apply(k))
                return items

            emit_p0_block(x_in, 0, xn_st, xbuf, xnb, ln_done, dxp2, dxnp2)
            SC.wait(C["dxn0"], C["dxn0"].n)
            SC.wait(C["dxn1"], C["dxn1"].n)
            for dc in range(16):
                SC.op("dma_start_transpose", xnT[:, dc * T:(dc + 1) * T],
                      xn_st[dc, :, :], inc=(C["dxnT"], 16))
            bg = mk_bg(0)
            emit_qkv_pass(st, own_jts, xnT, wqb, kbuf, kvbuf, qbuf, r2b,
                          tsrow, ps_qkv, ps_ts, do_stash=True, ts_base=0,
                          bg=bg)
            assert not bg

            def xnT_transposes(blk):
                SC.wait(C["dxn0"], C["dxn0"].n)
                SC.wait(C["dxn1"], C["dxn1"].n)
                SC.wait(C["mqkv"], C["mqkv"].n)
                for dc in range(16):
                    SC.op("dma_start_transpose", xnT[:, dc * T:(dc + 1) * T],
                          xst[blk % 2][dc, :, :],
                          inc=(C["dxnT"], 16))

            NBLK = NCORES - 1
            xnT_transposes(0)
            for blk in range(NBLK):
                bg = mk_bg(blk + 1) if blk + 1 < NBLK else []
                hook = (lambda b=blk: xnT_transposes(b + 1)) \
                    if blk + 1 < NBLK else None
                emit_qkv_pass(st, full_jts, xnT, wqb, kbuf, kvbuf, qbuf,
                              r2b, tsrow, ps_qkv, ps_ts, do_stash=False,
                              ts_base=(blk + 1) * NTT, bg=bg, tail_hook=hook)
                for f in bg:
                    f()

            n_stash = C["dstK"].n + C["dstKV"].n + C["dstQ"].n
            n_mqkv = C["mqkv"].n
            n_dts = C["dts0"].n + C["dts1"].n
            n_mev = C["mev"].n
            n_mtse = C["mtse"].n
            assert C["mqkv"].n == (12 + 8 * (NCORES - 1)) * NTT
            assert n_dts == 16 * NCORES * 8 * (NTT // 2)

        phase_barrier(["dw0", "dw1", "dstK", "dstKV", "dstQ", "dts0", "dts1",
                       "mqkv", "mev", "mtse", "fs", "fv", "dxnT", "dx0", "dx1",
                       "dxn0", "dxn1"])

        with contextlib.ExitStack() as p15:
            ts_sb = [p15.enter_context(nc.sbuf_tensor(f"tssb{i}", [SG, D], F32))
                     for i in range(2)]
            carf = [p15.enter_context(nc.sbuf_tensor(f"carf{i}", [NTT, D], F32))
                    for i in range(2)]
            carh = [p15.enter_context(nc.sbuf_tensor(f"carh{i}", [NTT, D], BF16))
                    for i in range(2)]
            carl = [p15.enter_context(nc.sbuf_tensor(f"carl{i}", [NTT, D], BF16))
                    for i in range(2)]
            SY.wait(C["dts0"], C["dts0"].n)
            SY.wait(C["dts1"], C["dts1"].n)
            for i in range(2):
                SY.op("dma_start", out=ts_sb[i][:, :], in_=ts_st[i, :, :],
                      inc=(C["dcar"], 16))
            TE.wait(C["mev"], n_mev)
            TE.wait(C["mtse"], n_mtse)
            with nc.psum_tensor("ps_pre0", [NTT, D], F32) as pp0, \
                    nc.psum_tensor("ps_pre1", [NTT, D], F32) as pp1:
                ps_pre = [pp0, pp1]
                for i in range(2):
                    TE.wait(C["dcar"], 32)
                    for sl in range(4):
                        TE.op("matmul", ps_pre[i][:, sl * 512:(sl + 1) * 512],
                              ps_sb[:, :], ts_sb[i][:, sl * 512:(sl + 1) * 512],
                              start=True, stop=True,
                              inc=(C["mts"], 1) if sl == 3 else None)
                    VE.wait(C["mts"], C["mts"].n)
                    v2 = VE.op("tensor_copy", carf[i][:, :], ps_pre[i][:, :],
                               inc=(C["v15"], 1))
                    VE.wait(C["v15"], v2)
                    v2h = VE.op("tensor_copy", carh[i][:, :], ps_pre[i][:, :],
                                inc=(C["v15"], 1))
                    VE.wait(C["v15"], v2h)
                    v2l = VE.op("tensor_sub", carl[i][:, :], ps_pre[i][:, :],
                                carh[i][:, :], inc=(C["v15"], 1))
                    SY.wait(C["v15"], v2l)
                    SY.op("dma_start", out=car_hi[i, :, :], in_=carh[i][:, :],
                          inc=(C["dcar"], 16))
                    SY.op("dma_start", out=car_lo[i, :, :], in_=carl[i][:, :],
                          inc=(C["dcar"], 16))
            n_dcar = C["dcar"].n
            n_v15 = C["v15"].n
            assert n_v15 == 6

        phase_barrier(["dcar", "v15", "mts"])

        with contextlib.ExitStack() as p2a:
            wob = p2a.enter_context(nc.sbuf_tensor("wob", [128, 16 * D], BF16))
            qt = [p2a.enter_context(nc.sbuf_tensor(f"qt{i}", [128, D], BF16))
                  for i in range(3)]
            kt = [p2a.enter_context(nc.sbuf_tensor(f"kt{i}", [128, D], BF16))
                  for i in range(3)]
            kvt = [p2a.enter_context(nc.sbuf_tensor(f"kvt{i}", [128, D], BF16))
                   for i in range(3)]
            crk = [p2a.enter_context(nc.sbuf_tensor(f"crk{i}", [1, 2 * D], BF16))
                   for i in range(2)]
            crkv = [p2a.enter_context(nc.sbuf_tensor(f"crkv{i}", [1, 2 * D], BF16))
                    for i in range(2)]
            t1s = [p2a.enter_context(nc.sbuf_tensor(f"t1s{i}", [128, 512], F32))
                   for i in range(2)]
            t2s = [p2a.enter_context(nc.sbuf_tensor(f"t2s{i}", [128, 512], F32))
                   for i in range(2)]
            lns = [p2a.enter_context(nc.sbuf_tensor(f"lns{i}", [128, 512], F32))
                   for i in range(2)]
            outb = [p2a.enter_context(nc.sbuf_tensor(f"outb{i}", [128, D], BF16))
                    for i in range(2)]
            outT = p2a.enter_context(nc.sbuf_tensor("outT", [128, 16 * 512], BF16))
            attnf = [p2a.enter_context(nc.sbuf_tensor(f"attnf{i}", [128, D], F32))
                     for i in range(2)]
            xrb = [p2a.enter_context(nc.sbuf_tensor("xrb0", [128, D], F32))] * 2
            ynb = p2a.enter_context(nc.sbuf_tensor("ynb", [128, D], BF16))
            ps_kc = [p2a.enter_context(nc.psum_tensor(f"ps_kc{i}", [128, 512], F32))
                     for i in range(2)]
            ps_kvc = [p2a.enter_context(nc.psum_tensor(f"ps_kvc{i}", [128, 512], F32))
                      for i in range(2)]
            ps_at = [p2a.enter_context(nc.psum_tensor(f"ps_at{i}", [128, 512], F32))
                     for i in range(2)]

            GP.wait(C["mqkv"], n_mqkv)
            GP.wait(C["dstK"], C["dstK"].n)
            GP.wait(C["dstKV"], C["dstKV"].n)
            GP.wait(C["dstQ"], C["dstQ"].n)
            GP.op("dma_start",
                  out=wob[:, :].rearrange("p (fc e) -> p fc e", e=D),
                  in_=woutT.rearrange("(fc p) e -> p fc e", p=128),
                  inc=(C["dwo"], 16))

            TE.wait(C["v15"], n_v15)

            att_G = 0
            dout_hist = {}


            b_hist = {}
            mcs_hist = {}
            vat_tile = {}
            ln_done_q = {}
            mult_done_q = {}
            pend = [None]
            first_mult_of_tile = {}

            def emit_mult(qg, mpar, mc0, dd, b):
                VE.wait(C["fs"], dd)
                VE.wait(C["fv"], b)
                if qg % 4 == 0:
                    mtile = qg // 4
                    if mtile >= 2:
                        doutp = C["dout0"] if mtile % 2 == 0 else C["dout1"]
                        VE.wait(doutp, doutp.n)
                v = VE.op("tensor_mul", outb[mpar][:, mc0:mc0 + 512],
                          t1s[qg % 2][:, :], lns[qg % 2][:, :],
                          inc=(C["vat"], 1))
                mult_done_q[qg] = v

            def emit_X_tile(ch, tl):
                if True:
                    tt = ch * 4 + tl
                    par = tt % 2
                    p3 = tt % 3
                    dsrp = [C["dsr0"], C["dsr1"], C["dsr2"]][p3]
                    dsgp = [C["dsg0"], C["dsg1"], C["dsg2"]][p3]
                    if tt >= 3:
                        SY.wait(C["vat"], vat_tile[tt - 3])
                        GP.wait(C["vat"], vat_tile[tt - 3])
                    GP.op("dma_start", out=qt[p3][:, :],
                          in_=q_st[tt * 128:(tt + 1) * 128, :], inc=(dsgp, 16))
                    SY.op("dma_start", out=kt[p3][:, :],
                          in_=k_st[tt * 128:(tt + 1) * 128, :], inc=(dsrp, 16))
                    GP.op("dma_start", out=kvt[p3][:, :],
                          in_=kv_st[tt * 128:(tt + 1) * 128, :],
                          inc=(dsgp, 16))
                    if tt >= 2:
                        SY.wait(C["mcs"], mcs_hist[tt - 1])
                    SY.op("dma_start", out=crk[par][0:1, 0:D],
                          in_=car_hi[0, tt:tt + 1, :], inc=(C["dcr"], 16))
                    SY.op("dma_start", out=crk[par][0:1, D:2 * D],
                          in_=car_lo[0, tt:tt + 1, :], inc=(C["dcr"], 16))
                    SY.op("dma_start", out=crkv[par][0:1, 0:D],
                          in_=car_hi[1, tt:tt + 1, :], inc=(C["dcr"], 16))
                    SY.op("dma_start", out=crkv[par][0:1, D:2 * D],
                          in_=car_lo[1, tt:tt + 1, :], inc=(C["dcr"], 16))
                    nds = dsrp.n
                    ndsg = dsgp.n
                    ndcr = C["dcr"].n

                    for q in range(4):
                        qg = emit_X_tile.qg
                        qs = qg % 2
                        c0 = q * 512
                        TE.wait(dsrp, nds)
                        TE.wait(dsgp, ndsg)
                        TE.wait(C["dcr"], ndcr)
                        if qg - 2 in b_hist:
                            TE.wait(C["fv"], b_hist[qg - 2])
                        TE.op("matmul", ps_kc[qs][:, :],
                              u_sb[:, :], kt[p3][:, c0:c0 + 512],
                              start=True, stop=False)
                        TE.op("matmul", ps_kc[qs][:, :],
                              u_sb[0:1, :], crk[par][0:1, c0:c0 + 512],
                              start=False, stop=False)
                        TE.op("matmul", ps_kc[qs][:, :],
                              u_sb[0:1, :], crk[par][0:1, D + c0:D + c0 + 512],
                              start=False, stop=True)
                        TE.op("matmul", ps_kvc[qs][:, :],
                              u_sb[:, :], kvt[p3][:, c0:c0 + 512],
                              start=True, stop=False)
                        TE.op("matmul", ps_kvc[qs][:, :],
                              u_sb[0:1, :], crkv[par][0:1, c0:c0 + 512],
                              start=False, stop=False)
                        TE.op("matmul", ps_kvc[qs][:, :],
                              u_sb[0:1, :], crkv[par][0:1, D + c0:D + c0 + 512],
                              start=False, stop=True, inc=(C["mcs"], 1))
                        mnow = C["mcs"].n

                        if pend[0] is not None:
                            emit_mult(*pend[0])
                        VE.wait(C["mcs"], mnow)
                        VE.wait(dsgp, ndsg)
                        if qg - 2 in ln_done_q:
                            VE.wait(C["fs"], ln_done_q[qg - 2])
                        if qg - 2 in mult_done_q:
                            VE.wait(C["vat"], mult_done_q[qg - 2])
                        a = VE.op("scalar_tensor_tensor", t2s[qs][:, :],
                                  ps_kc[qs][:, :], 1.0,
                                  qt[p3][:, c0:c0 + 512],
                                  ALU.mult, ALU.mult, inc=(C["fv"], 1))
                        b = VE.op("scalar_tensor_tensor", t1s[qs][:, :],
                                  ps_kvc[qs][:, :], 1.0,
                                  qt[p3][:, c0:c0 + 512],
                                  ALU.mult, ALU.mult, inc=(C["fv"], 1))
                        b_hist[qg] = b
                        SC.wait(C["fv"], a)
                        if qg - 2 in mult_done_q:
                            SC.wait(C["vat"], mult_done_q[qg - 2])
                        cc = SC.op("activation", lns[qs][:, :], t2s[qs][:, :],
                                   AF.Ln, bias=np.float32(ATT_EPS),
                                   inc=(C["fs"], 1))
                        SC.wait(C["fs"], cc)
                        dd = SC.op("activation", lns[qs][:, :], lns[qs][:, :],
                                   AF.Exp, scale=-1.0, inc=(C["fs"], 1))
                        ln_done_q[qg] = dd
                        pend[0] = (qg, par, c0, dd, b)
                        emit_X_tile.qg += 1
                    mcs_hist[tt] = C["mcs"].n
                    if pend[0] is not None:
                        emit_mult(*pend[0])
                        pend[0] = None
                    vat_tile[tt] = C["vat"].n
                    SY.wait(C["vat"], C["vat"].n)
                    doutp = C["dout0"] if par == 0 else C["dout1"]
                    SY.op("dma_start", out=out_st[tt * 128:(tt + 1) * 128, :],
                          in_=outb[par][:, :], inc=(doutp, 16))
                if tl == 3:
                    dout_hist[ch] = (C["dout0"].n, C["dout1"].n)

            emit_X_tile.qg = 0
            ln_hist = {}
            vy_hist = {}

            def emit_Y_head(ch):
                SC.wait(C["dout0"], dout_hist[ch][0])
                SC.wait(C["dout1"], dout_hist[ch][1])
                for fc in range(16):
                    SC.op("dma_start_transpose", outT[:, fc * 512:(fc + 1) * 512],
                          out_st[ch * 512:(ch + 1) * 512,
                                 fc * 128:(fc + 1) * 128],
                          inc=(C["doutT"], 16))
                emit_Y_head.n_doutT = C["doutT"].n

            def emit_Y_tile(ch, tl):
                n_doutT = emit_Y_head.n_doutT
                if True:
                    tt = ch * 4 + tl
                    VE_done_for_xrb = C["fv"].n
                    SY.wait(C["fv"], VE_done_for_xrb)
                    SY.wait(C["fs"], C["fs"].n)
                    SY.wait(C["dy"], C["dy"].n)
                    SY.op("dma_start", out=xrb[0][:, :],
                          in_=x_in[tt * 128:(tt + 1) * 128, :],
                          inc=(C["dxr"], 16))
                    ndsx = C["dxr"].n
                    for es_ in range(4):
                        G = emit_Y_tile.att_G
                        apar = G % 2
                        TE.wait(C["doutT"], n_doutT)
                        TE.wait(C["dwo"], 16)
                        if G >= 2:
                            TE.wait(C["sat"], G - 1)
                        for fc in range(16):
                            TE.op("matmul", ps_at[apar][:, :],
                                  outT[:, fc * 512 + tl * 128:
                                       fc * 512 + (tl + 1) * 128],
                                  wob[:, fc * D + es_ * 512:
                                      fc * D + (es_ + 1) * 512],
                                  start=(fc == 0), stop=(fc == 15),
                                  inc=(C["mat"], 1) if fc == 15 else None)
                        SC.wait(C["mat"], C["mat"].n)
                        if es_ == 0 and tt >= 2:
                            SC.wait(C["fv"], vy_hist[tt - 2])
                        SC.op("activation",
                              attnf[tt % 2][:, es_ * 512:(es_ + 1) * 512],
                              ps_at[apar][:, :], AF.Copy, bias=0.0,
                              scale=np.float32(HD_SCALE), inc=(C["sat"], 1))
                        emit_Y_tile.att_G += 1
                    VE.wait(C["sat"], C["sat"].n)
                    ln1 = ln_stats_apply(attnf[tt % 2][:, :], attnf[tt % 2][:, :])
                    ln_hist[tt] = ln1
                    VE.wait(C["fs"], ln1)
                    VE.wait(C["dxr"], ndsx)
                    vy = VE.op("tensor_add", xrb[0][:, :], xrb[0][:, :],
                               attnf[tt % 2][:, :], inc=(C["fv"], 1))
                    vy_hist[tt] = vy
                    SY.wait(C["fv"], vy)
                    SY.op("dma_start", out=y_st[tt * 128:(tt + 1) * 128, :],
                          in_=xrb[0][:, :], inc=(C["dy"], 16))
                    VE.wait(C["fv"], vy)
                    SC.wait(C["fv"], vy)
                    SC.wait(C["dyn"], C["dyn"].n)
                    ln2 = ln_stats_apply(xrb[0][:, :], ynb[:, :])
                    SY.wait(C["fs"], ln2)
                    SY.op("dma_start", out=yn_st[tt * 128:(tt + 1) * 128, :],
                          in_=ynb[:, :], inc=(C["dyn"], 16))
            emit_Y_tile.att_G = 0

            for tl in range(4):
                emit_X_tile(0, tl)
            for ch in range(NCH):
                emit_Y_head(ch)
                for tl in range(4):
                    if ch + 1 < NCH:
                        emit_X_tile(ch + 1, tl)
                    emit_Y_tile(ch, tl)
            n_dy = C["dy"].n
            n_dyn = C["dyn"].n
            n_fv_2a = C["fv"].n
            n_sat = C["sat"].n
            assert n_dy == 16 * NTT and n_dyn == 16 * NTT

        phase_barrier(["dsr0", "dsr1", "dsr2", "dsg0", "dsg1", "dsg2",
                       "dcr", "dxr", "dout0", "dout1",
                       "doutT", "dy", "dyn", "dwo", "mcs",
                       "mat", "sat", "vat", "fs", "fv"])

        with contextlib.ExitStack() as p2b:
            ynT = p2b.enter_context(nc.sbuf_tensor("ynT", [128, 16 * 512], BF16))
            wgb = [p2b.enter_context(
                nc.sbuf_tensor(f"wgb{i}", [128, 16 * 256], BF16)) for i in range(2)]
            wub = [p2b.enter_context(
                nc.sbuf_tensor(f"wub{i}", [128, 16 * 256], BF16)) for i in range(2)]
            wdb = [p2b.enter_context(
                nc.sbuf_tensor(f"wdb{i}", [128, 4 * 1024], BF16)) for i in range(2)]
            fuse = p2b.enter_context(nc.sbuf_tensor("fuse", [128, 64 * 512], BF16))
            silub = [p2b.enter_context(
                nc.sbuf_tensor(f"silub{i}", [128, 512], F32)) for i in range(2)]
            sdb = p2b.enter_context(nc.sbuf_tensor("sdb", [128, D], F32))
            zf = [p2b.enter_context(nc.sbuf_tensor(f"zf{i}", [128, D], F32))
                  for i in range(4)]
            zlnb = p2b.enter_context(nc.sbuf_tensor("zlnb", [128, D], F32))
            yrb = p2b.enter_context(nc.sbuf_tensor("yrb", [128, D], F32))
            sd_sb = p2b.enter_context(nc.sbuf_tensor("sd_sb", [1, D], F32))
            SY.op("dma_start", out=sd_sb[:, :], in_=sd_in, inc=(C["dsd"], 16))

            TE.wait(C["fv"], n_fv_2a)
            TE.wait(C["sat"], n_sat)
            TE.wait(C["dsd"], 16)
            with nc.psum_tensor("ps_sd", [128, D], F32) as ps_sd:
                for sl in range(4):
                    TE.op("matmul", ps_sd[:, sl * 512:(sl + 1) * 512],
                          ones1_sb[0:1, :], sd_sb[0:1, sl * 512:(sl + 1) * 512],
                          start=True, stop=True,
                          inc=(C["mts"], 1) if sl == 3 else None)
                VE.wait(C["mts"], C["mts"].n)
                VE.op("tensor_copy", sdb[:, :], ps_sd[:, :], inc=(C["vz"], 1))

            mg_hist = []
            g_glob = 0
            ocg_glob = 0
            eh_glob = 0

            def emit_ynT_transposes(ch):
                SC.wait(C["dyn"], n_dyn)
                SC.wait(C["mg"], C["mg"].n)
                for fc in range(16):
                    SC.op("dma_start_transpose", ynT[:, fc * 512:(fc + 1) * 512],
                          yn_st[ch * 512:(ch + 1) * 512,
                                fc * 128:(fc + 1) * 128],
                          inc=(C["dynT"], 16))
                return C["dynT"].n

            dynT_hist = [emit_ynT_transposes(0)]
            for ch in range(NCH):
                n_dynT = dynT_hist[ch]

                stA = contextlib.ExitStack()
                ps_g = [stA.enter_context(
                    nc.psum_tensor(f"ps_g{ch}_{i}", [128, 512], F32))
                    for i in range(2)]
                ps_u = [stA.enter_context(
                    nc.psum_tensor(f"ps_u{ch}_{i}", [128, 512], F32))
                    for i in range(2)]
                if ch > 0:
                    VE.wait(C["mzb"], 32 * ch)
                TE.wait(C["vz"], C["vz"].n)
                for om in range(32):
                    om_glob = ch * 32 + om
                    par = om % 2
                    if om_glob >= 2:
                        GP.wait(C["mg"], mg_hist[om_glob - 2])
                    srcg = wg_in.rearrange("(dc p) o -> p dc o", p=128)[
                        :, :, om * 256:(om + 1) * 256]
                    srcu = wu_in.rearrange("(dc p) o -> p dc o", p=128)[
                        :, :, om * 256:(om + 1) * 256]
                    dwgp = C["dwg0"] if par == 0 else C["dwg1"]
                    GP.op("dma_start",
                          out=wgb[par][:, :].rearrange("p (dc o) -> p dc o", o=256),
                          in_=srcg, inc=(dwgp, 16))
                    GP.op("dma_start",
                          out=wub[par][:, :].rearrange("p (dc o) -> p dc o", o=256),
                          in_=srcu, inc=(dwgp, 16))
                    ndw = dwgp.n
                    for oc4 in range(2):
                        go = om * 2 + oc4
                        gpar = g_glob % 2
                        TE.wait(dwgp, ndw)
                        TE.wait(C["dynT"], n_dynT)
                        if g_glob >= 2:
                            TE.wait(C["sfa"], g_glob - 1)
                            TE.wait(C["vfa"], g_glob - 1)
                        for dc in range(16):
                            TE.op("matmul", ps_g[gpar][:, :],
                                  wgb[par][:, dc * 256 + oc4 * 128:
                                           dc * 256 + (oc4 + 1) * 128],
                                  ynT[:, dc * 512:(dc + 1) * 512],
                                  start=(dc == 0), stop=(dc == 15))
                        for dc in range(16):
                            TE.op("matmul", ps_u[gpar][:, :],
                                  wub[par][:, dc * 256 + oc4 * 128:
                                           dc * 256 + (oc4 + 1) * 128],
                                  ynT[:, dc * 512:(dc + 1) * 512],
                                  start=(dc == 0), stop=(dc == 15),
                                  inc=(C["mg"], 1) if dc == 15 else None)
                        mgn = C["mg"].n
                        SC.wait(C["mg"], mgn)
                        if g_glob >= 2:
                            SC.wait(C["vfa"], g_glob - 1)
                        sa = SC.op("activation", silub[gpar][:, :],
                                   ps_g[gpar][:, :], AF.Silu,
                                   scale=sg_sb[:, go:go + 1], inc=(C["sfa"], 1))
                        VE.wait(C["sfa"], sa)
                        VE.op("scalar_tensor_tensor",
                              fuse[:, go * 512:(go + 1) * 512],
                              ps_u[gpar][:, :], su_sb[:, go:go + 1],
                              silub[gpar][:, :], ALU.mult, ALU.mult,
                              inc=(C["vfa"], 1))
                        g_glob += 1
                    mg_hist.append(C["mg"].n)
                nvfa = C["vfa"].n
                if ch + 1 < NCH:
                    dynT_hist.append(emit_ynT_transposes(ch + 1))
                stA.close()
                stB = contextlib.ExitStack()
                ps_z = [stB.enter_context(
                    nc.psum_tensor(f"ps_z{ch}_{i}", [128, 1024], F32))
                    for i in range(4)]

                for eh in range(2):
                    for ocg in range(16):
                        par = ocg_glob % 2
                        if ocg_glob >= 2:
                            GP.wait(C["mzb"], ocg_glob - 1)
                        srcd = wd_in.rearrange("(g p) e -> p g e", p=128)[
                            :, ocg * 4:(ocg + 1) * 4,
                            eh * 1024:(eh + 1) * 1024]
                        dwdp = C["dwd0"] if par == 0 else C["dwd1"]
                        GP.op("dma_start",
                              out=wdb[par][:, :].rearrange(
                                  "p (g e) -> p g e", e=1024),
                              in_=srcd, inc=(dwdp, 16))
                        ndd = dwdp.n
                        TE.wait(dwdp, ndd)
                        TE.wait(C["vfa"], nvfa)
                        if ocg == 0:
                            TE.wait(C["vz"], C["vz"].n)
                        for oc4 in range(4):
                            oc = ocg * 4 + oc4
                            for tl in range(4):
                                for es_ in range(2):
                                    last = (oc4 == 3 and tl == 3 and es_ == 1)
                                    TE.op("matmul",
                                          ps_z[tl][:, es_ * 512:(es_ + 1) * 512],
                                          fuse[:, oc * 512 + tl * 128:
                                               oc * 512 + (tl + 1) * 128],
                                          wdb[par][:, oc4 * 1024 + es_ * 512:
                                                   oc4 * 1024 + (es_ + 1) * 512],
                                          start=(oc == 0), stop=(oc == 63),
                                          inc=(C["mzb"], 1) if last else None)
                        ocg_glob += 1
                    mzn = C["mzb"].n
                    for tl in range(4):
                        VE.wait(C["mzb"], mzn)
                        if eh == 0 and ch > 0:
                            VE.wait(C["fs"], C["fs"].n)
                        VE.op("scalar_tensor_tensor",
                              zf[tl][:, eh * 1024:(eh + 1) * 1024],
                              ps_z[tl][:, :],
                              1.0, sdb[:, eh * 1024:(eh + 1) * 1024],
                              ALU.mult, ALU.mult, inc=(C["vz"], 1))
                    eh_glob += 1
                stB.close()

                for tl in range(4):
                    tt = ch * 4 + tl
                    SY.wait(C["fv"], C["fv"].n)
                    SY.wait(C["dfin"], C["dfin"].n)
                    SY.op("dma_start", out=yrb[:, :],
                          in_=y_st[tt * 128:(tt + 1) * 128, :], inc=(C["dyr"], 16))
                    ndsy = C["dyr"].n
                    VE.wait(C["vz"], C["vz"].n)
                    SC.wait(C["fv"], C["fv"].n)
                    ln3 = ln_stats_apply(zf[tl][:, :], zlnb[:, :])
                    VE.wait(C["fs"], ln3)
                    VE.wait(C["dyr"], ndsy)
                    vf = VE.op("tensor_add", yrb[:, :], yrb[:, :], zlnb[:, :],
                               inc=(C["fv"], 1))
                    SY.wait(C["fv"], vf)
                    SY.op("dma_start", out=y_out[tt * 128:(tt + 1) * 128, :],
                          in_=yrb[:, :], inc=(C["dfin"], 16))
            assert C["dfin"].n == 16 * NTT

        SY.wait(C["dfin"], 16 * NTT)

        with nc.Block() as block:
            @block.sync
            def _(e):
                recs["sync"].replay(e)

            @block.tensor
            def _(e):
                recs["tensor"].replay(e)

            @block.scalar
            def _(e):
                recs["scalar"].replay(e)

            @block.vector
            def _(e):
                recs["vector"].replay(e)

            @block.gpsimd
            def _(e):
                recs["gpsimd"].replay(e)

    return nc


def _host_consts(T):
    import ml_dtypes
    u = np.triu(np.ones((128, 128))).astype(ml_dtypes.bfloat16)
    ones1f = np.ones((1, 128), dtype=np.float32)
    return u, ones1f


def _make_in_maps(x, wqkvT, woutT, wg, wu, wd, sg_c, su_c, sd_row, T):
    u, ones1f = _host_consts(T)
    NTT = T // 128
    SG = NCORES * NTT
    in_maps = []
    for c in range(NCORES):
        gmap = np.empty(SG, dtype=np.int64)
        gmap[0:NTT] = NTT * c + np.arange(NTT)
        fb = [b for b in range(NCORES) if b != c]
        for i, b in enumerate(fb):
            gmap[NTT + i * NTT: NTT + (i + 1) * NTT] = NTT * b + np.arange(NTT)
        ps_sel = (gmap[:, None] < (NTT * c + np.arange(NTT))[None, :]
                  ).astype(np.float32)
        x_pre = np.ascontiguousarray(
            np.concatenate([x[b * T:(b + 1) * T] for b in fb], axis=0))
        in_maps.append(dict(
            x_sh=np.ascontiguousarray(x[c * T:(c + 1) * T]),
            x_full=x_pre,
            wqkvT=wqkvT, woutT=woutT, wg=wg, wu=wu, wd=wd,
            sg_c=sg_c, su_c=su_c, sd_row=sd_row,
            u_tri=u, ones1f=ones1f, ps_sel=ps_sel,
        ))
    return in_maps


def kernel(L0_attn_x, L0_attn_W_qkv, L0_attn_W_out, L0_ffn_w_gate, L0_ffn_w_up,
           L0_ffn_w_down, L0_ffn_s_gate, L0_ffn_s_up, L0_ffn_s_down):
    T = 16384 // NCORES
    x = np.asarray(L0_attn_x).reshape(16384, D)
    wqkvT = np.ascontiguousarray(np.asarray(L0_attn_W_qkv).T)
    woutT = np.ascontiguousarray(np.asarray(L0_attn_W_out).T)
    wg = np.ascontiguousarray(np.asarray(L0_ffn_w_gate))
    wu = np.ascontiguousarray(np.asarray(L0_ffn_w_up))
    wd = np.ascontiguousarray(np.asarray(L0_ffn_w_down))
    sg_c = np.ascontiguousarray(
        np.asarray(L0_ffn_s_gate).reshape(DI // 128, 128).T).astype(np.float32)
    su_c = np.ascontiguousarray(
        np.asarray(L0_ffn_s_up).reshape(DI // 128, 128).T).astype(np.float32)
    sd_row = np.ascontiguousarray(
        np.asarray(L0_ffn_s_down).reshape(1, D)).astype(np.float32)

    nc = build_nc(T)
    in_maps = _make_in_maps(x, wqkvT, woutT, wg, wu, wd, sg_c, su_c, sd_row, T)
    res = run_bass_kernel_spmd(nc, in_maps, core_ids=list(range(NCORES)))
    out = np.concatenate([res.results[c]["y_out"] for c in range(NCORES)], axis=0)
    return out.reshape(1, 16384, D).astype(np.float32)
